# version 22
# baseline (speedup 1.0000x reference)
import numpy as np
import concourse.bass as bass
import concourse.mybir as mybir
from concourse.bass_utils import run_bass_kernel_spmd
from contextlib import ExitStack

F32 = mybir.dt.float32
BF16 = mybir.dt.bfloat16
AF = mybir.ActivationFunctionType
OP = mybir.AluOpType
AX = mybir.AxisListType

LN_EPS = 1e-5
RMS_EPS = float(np.finfo(np.float32).eps)
TOP_P = 0.3


class Cfg:
    def __init__(self, n_cores=8, ntok=4096, G=512, stop_after=None):
        self.n_cores = n_cores
        self.ntok = ntok
        self.G = G
        self.C = 128
        self.H = 16
        self.DH = 64
        self.D = 1024
        self.E = 8
        self.n_groups = ntok // G
        self.TT = G // 128
        self.stop_after = stop_after
        self.pairs = [[i, i + 1] for i in range(0, n_cores, 2)]


class P:

    def __init__(self, nc, sems):
        self.nc = nc
        self.sems = sems
        self.active = None
        self.eng = None
        self.c = None

    def begin(self, active, eng):
        self.active = active
        self.eng = eng
        self.c = {k: 0 for k in self.sems}
        self.last = None

    def inc(self, sem, n=1):
        self.c[sem] += n
        return self.c[sem]

    def do(self, engine, fn, inc=None, n=1):
        if self.active == engine:
            inst = fn(self.eng)
            self.last = inst
            if inc is not None:
                inst.then_inc(self.sems[inc], n)
        if inc is not None:
            return self.inc(inc, n)
        return None

    def mark(self, engine, sem, n=1):
        if self.active == engine:
            self.last.then_inc(self.sems[sem], n)
        return self.inc(sem, n)

    def wait(self, engine, sem, val):
        assert val <= self.c[sem], (sem, val, self.c[sem])
        if self.active == engine:
            self.eng.wait_ge(self.sems[sem], val)

    def wait_future(self, engine, sem, val):
        if self.active == engine:
            self.eng.wait_ge(self.sems[sem], val)

    def drain_inc(self, engine, sem, n=1):
        if self.active == engine:
            self.eng.drain().then_inc(self.sems[sem], n)
        return self.inc(sem, n)

    def drain(self, engine):
        if self.active == engine:
            self.eng.drain()


def build(cfg: Cfg) -> bass.Bass:
    nc = bass.Bass("TRN2", target_bir_lowering=False)
    NT, G, TT, H, DH, D, E = cfg.ntok, cfg.G, cfg.TT, cfg.H, cfg.DH, cfg.D, cfg.E
    NG = cfg.n_groups
    NCH = NT // 128

    x_in = nc.dram_tensor("x", [NT, D], F32, kind="ExternalInput").ap()
    y_out = nc.dram_tensor("y", [NT, D], F32, kind="ExternalOutput").ap()
    wmats = {}
    for nm in ("wq", "wk", "wv", "wg", "wog", "wo", "e0", "e1", "ow"):
        wmats[nm] = nc.dram_tensor(nm, [D, D], F32, kind="ExternalInput").ap()
    gw = nc.dram_tensor("gw", [D, E], F32, kind="ExternalInput").ap()
    wd = nc.dram_tensor("wd", [D, H], F32, kind="ExternalInput").ap()
    bd_in = nc.dram_tensor("bd", [H, 1], F32, kind="ExternalInput").ap()
    nwb_in = nc.dram_tensor("normw_b", [128, D], F32, kind="ExternalInput").ap()
    i128_in = nc.dram_tensor("i128", [128, 128], F32, kind="ExternalInput").ap()
    tri_in = nc.dram_tensor("tri", [128, 128], F32, kind="ExternalInput").ap()
    ones_in = nc.dram_tensor("ones1", [1, 128], F32, kind="ExternalInput").ap()
    ones16_in = nc.dram_tensor("ones16", [16, 128], F32, kind="ExternalInput").ap()
    cmask_in = nc.dram_tensor("cmask", [128, 1], F32, kind="ExternalInput").ap()
    umask_in = nc.dram_tensor("umask", [128, 1], F32, kind="ExternalInput").ap()
    consts_in = nc.dram_tensor("consts", [128, 4], F32, kind="ExternalInput").ap()

    sp_x1 = nc.dram_tensor("sp_x1", [NT, D], F32).ap()
    sp_lnFh = nc.dram_tensor("sp_lnFh", [D, NT], BF16).ap()
    sp_lnFl = nc.dram_tensor("sp_lnFl", [D, NT], BF16).ap()
    whi_d = {}
    wlo_d = {}
    for nm in ("wq", "wk", "wv", "wg", "wog", "wo", "e0", "e1", "ow"):
        whi_d[nm] = nc.dram_tensor("whi_" + nm, [D, D], BF16).ap()
        wlo_d[nm] = nc.dram_tensor("wlo_" + nm, [D, D], BF16).ap()
    sp_kgF = nc.dram_tensor("sp_kgF", [D, NT], F32).ap()
    sp_kgT = nc.dram_tensor("sp_kgT", [NT, D], F32).ap()
    sp_v = nc.dram_tensor("sp_v", [NT, D], F32).ap()
    sp_m = nc.dram_tensor("sp_m", [H, NT], F32).ap()
    cc_in = nc.dram_tensor("cc_in", [128, (H // 2) * DH], F32).ap()
    cc_out = nc.dram_tensor("cc_out", [128, (H // 2) * DH], F32).ap()

    with ExitStack() as ctx:
        sb = lambda name, shape, dt=F32: ctx.enter_context(nc.sbuf_tensor(name, shape, dt))
        psb = lambda name, shape: ctx.enter_context(nc.psum_tensor(name, shape, F32))

        xd = sb("xd", [128, 2 * TT * D])
        b_lnFh = sb("b_lnFh", [128, 8 * G], BF16)
        b_lnFl = sb("b_lnFl", [128, 8 * G], BF16)
        b_lnF = sb("b_lnF32", [128, 8 * G])
        b_aux = sb("b_aux", [128, 8 * G])
        b_kgF = sb("b_kgF", [128, 8 * G])
        b_kgT = sb("b_kgT", [128, TT * D])
        b_v = sb("b_v", [128, TT * D])
        b_og = b_lnF
        ln_tm = sb("ln_tm", [128, TT * D])
        b_f = ln_tm
        o_tm = sb("o_tm", [128, D])
        wh0 = sb("wh0", [128, 8 * 512], BF16)
        wl0 = sb("wl0", [128, 8 * 512], BF16)
        wh1 = sb("wh1", [128, 8 * 512], BF16)
        wl1 = sb("wl1", [128, 8 * 512], BF16)
        wslots = [(wh0, wl0), (wh1, wl1)]
        gw_sb = sb("gw_sb", [128, 8 * E])
        wd_sb = sb("wd_sb", [128, 8 * H])
        bd_sb = sb("bd_sb", [H, 1])
        nwb = sb("nwb", [128, D])
        i128 = sb("i128_sb", [128, 128])
        tri = sb("tri_sb", [128, 128])
        ones1 = sb("ones_sb", [1, 128])
        ones16 = sb("ones16_sb", [16, 128])
        mc_rhs = sb("mc_rhs", [H, 64])
        cmask = sb("cmask_sb", [128, 1])
        umask = sb("umask_sb", [128, 1])
        csb = sb("consts_sb", [128, 4])
        s_a = sb("s_a", [128, (H // 2) * DH])
        s_b = sb("s_b", [128, (H // 2) * DH])
        s_bufs = [s_a, s_b]
        m_gr = sb("m_gr", [H, G])
        m_tm = sb("m_tm", [128, TT * H])
        t_alpha = sb("t_alpha", [128, TT * H])
        t_inva = sb("t_inva", [128, TT * H])
        t_beta = sb("t_beta", [128, TT * H])
        t_lam = sb("t_lam", [128, TT * H])
        St4 = sb("St4", [128, 512])
        kgb_all = sb("kgb_all", [128, D])
        kgb_bufs = (kgb_all, o_tm)
        zscr = sb("zscr", [128, 4 * E])
        s0s1 = sb("s0s1", [128, 2 * TT])
        stat = sb("stat", [128, 24])
        burn = None
        sp_scr = sb("sp_scr", [H, 2 * G])

        ps_mm = psb("ps_mm", [128, 2048])
        ps_tr = psb("ps_tr", [128, 512])
        ps_sc = psb("ps_sc", [128, 512])
        ps_o = psb("ps_o", [128, 512])
        ps_s = psb("ps_s", [128, 512])

        sems = {}
        for s in ("qw0", "qw1", "qx0", "qx1", "qp0", "qp1", "qs", "qs2", "qs3", "qg", "t", "v", "a", "tw", "cc"):
            sems[s] = ctx.enter_context(nc.semaphore(s))
        block = ctx.enter_context(nc.Block())
        p = P(nc, sems)

        a_ms = ["e0", "e1", "ow", "wg", "wk", "wv"]
        if cfg.stop_after == "ffn1":
            a_ms = ["e0", "e1", "ow"]
        b_ms = ["wq", "wog", "wo", "e0", "e1", "ow"]
        if cfg.stop_after in ("ffn1", "proj"):
            b_ms = []
        wload_seq = []
        for g in range(NG):
            for mname in a_ms:
                wload_seq.append((mname, 0))
                wload_seq.append((mname, 1))
        for g in range(NG):
            for mname in b_ms:
                wload_seq.append((mname, 0))
                wload_seq.append((mname, 1))

        ws = {"loaded": 0, "used": 0, "tfree": 0}

        def w_dram_half(dram, half):
            apx = dram[:, half * 512:(half + 1) * 512]
            return apx.rearrange("(k p) c -> p k c", p=128)

        def wslot_view(buf):
            return buf[:].rearrange("p (k c) -> p k c", c=512)

        def w_issue_one():
            k = ws["loaded"]
            if k >= len(wload_seq):
                return
            mname, half = wload_seq[k]
            sem = "qw0" if k % 2 == 0 else "qw1"
            wh, wl = wslots[k % 2]
            p.do("sync", lambda e, wh=wh, mname=mname, half=half: e.dma_start(
                out=wslot_view(wh), in_=w_dram_half(whi_d[mname], half)), inc=sem, n=16)
            p.wait("scalar", "t", ws["tfree"])
            p.do("scalar", lambda e, wl=wl, mname=mname, half=half: e.dma_start(
                out=wslot_view(wl), in_=w_dram_half(wlo_d[mname], half)), inc=sem, n=16)
            ws["loaded"] += 1

        def w_use_begin(k):
            base = ws["qw0_base"] if k % 2 == 0 else ws["qw1_base"]
            p.wait_future("tensor", "qw0" if k % 2 == 0 else "qw1",
                          base + 32 * (k // 2 + 1))
            return wslots[k % 2]

        def w_use_end(k):
            ws["used"] += 1
            ws["tfree"] = p.c["t"]
            p.wait("sync", "t", p.c["t"])
            w_issue_one()

        def dma(qeng, out, in_, sem):
            p.do(qeng, lambda e, out=out, in_=in_: e.dma_start(out=out, in_=in_), inc=sem, n=16)

        def xgrp(slot):
            return xd[:, slot * TT * D:(slot + 1) * TT * D]

        def x_tt(slot, tt):
            return xd[:, slot * TT * D + tt * D: slot * TT * D + (tt + 1) * D]

        def tokmaj_dram_view(dram, g):
            return dram[g * G:(g + 1) * G, :].rearrange("(t p) c -> p t c", p=128)

        def featmaj_dram_view(dram, g):
            return dram[:, g * G:(g + 1) * G].rearrange("(ch p) t -> p ch t", p=128)

        def featmaj_sb_view(buf):
            return buf[:].rearrange("p (ch t) -> p ch t", t=G)

        def tokmaj_sb_view(buf):
            return buf[:, 0:TT * D].rearrange("p (t c) -> p t c", c=D)

        def weight_prologue():
            names = ["e0", "e1", "ow", "wg", "wk", "wv", "wq", "wog", "wo"]
            steps = [(nm, r) for nm in names for r in range(8)]
            flat = (wslots[0][0], wslots[0][1], wslots[1][0], wslots[1][1])
            npar = min(4, (8 * G) // 1024)
            qws = ("qw0", "qw1", "qx0", "qx1")[:npar]
            qps = ("qp0", "qp1", "qs", "qs2")[:npar]
            qw_base = [p.c[s] for s in qws]
            qp_base = [p.c[s] for s in qps]
            vmarks = []
            for i, (nm, r) in enumerate(steps):
                par = i % npar
                qw = qws[par]
                qp = qps[par]
                f32buf = b_aux[:, par * 1024:(par + 1) * 1024]
                hibuf = flat[par][:, 0:1024]
                lobuf = flat[par][:, 1024:2048]
                rows = wmats[nm][r * 128:(r + 1) * 128, :]
                if i >= npar:
                    p.wait("sync", "v", vmarks[i - npar])
                p.do("sync", lambda e, f32buf=f32buf, rows=rows: e.dma_start(
                    out=f32buf.rearrange("p (k c) -> p k c", c=D),
                    in_=rows.rearrange("(k p) c -> p k c", p=128)), inc=qw, n=16)
                qv = qw_base[par] + 16 * (i // npar + 1)
                pv = qp_base[par] + 32 * (i // npar)
                p.wait("scalar", qw, qv)
                p.wait("scalar", qp, pv)
                p.do("scalar", lambda e, hibuf=hibuf, f32buf=f32buf: e.copy(
                    out=hibuf, in_=f32buf))
                am = p.mark("scalar", "a")
                p.wait("vector", "a", am)
                p.wait("vector", qp, pv)
                p.do("vector", lambda e, lobuf=lobuf, f32buf=f32buf, hibuf=hibuf: e.tensor_tensor(
                    out=lobuf, in0=f32buf, in1=hibuf, op=OP.subtract))
                mk = p.mark("vector", "v")
                vmarks.append(mk)
                p.wait("gpsimd", "v", mk)
                dsth = whi_d[nm][r * 128:(r + 1) * 128, :]
                dstl = wlo_d[nm][r * 128:(r + 1) * 128, :]
                dma("gpsimd", dsth.rearrange("(k p) c -> p k c", p=128),
                    hibuf.rearrange("p (k c) -> p k c", c=D), qp)
                dma("gpsimd", dstl.rearrange("(k p) c -> p k c", p=128),
                    lobuf.rearrange("p (k c) -> p k c", c=D), qp)
            for s in qps:
                p.wait("sync", s, p.c[s])
                p.wait("scalar", s, p.c[s])

        def ln_tile(slot, tt):
            p.wait("vector", "v", p.c["v"])
            p.wait("scalar", "v", p.c["v"])
            xa = x_tt(slot, tt)
            p.do("vector", lambda e, xa=xa: e.bn_stats(
                out=stat[:, 0:6], in_=xa[:, 0:512]))
            p.do("vector", lambda e, xa=xa: e.bn_stats(
                out=stat[:, 6:12], in_=xa[:, 512:1024]))
            p.drain("vector")
            c0 = 12 + 3 * tt
            p.do("vector", lambda e, c0=c0: e.bn_aggr(out=stat[:, c0:c0 + 2], in_=stat[:, 0:12]))
            mkv = p.mark("vector", "v")
            p.wait("scalar", "v", mkv)
            p.do("scalar", lambda e, c0=c0: e.activation(
                out=stat[:, c0 + 1:c0 + 2], in_=stat[:, c0 + 1:c0 + 2], func=AF.Sqrt,
                bias=csb[:, 0:1], scale=1.0))
            am = p.mark("scalar", "a")
            p.wait("vector", "a", am)
            p.do("vector", lambda e, c0=c0: e.reciprocal(
                out=stat[:, c0 + 1:c0 + 2], in_=stat[:, c0 + 1:c0 + 2]))
            p.drain("vector")
            p.do("vector", lambda e, c0=c0: e.tensor_scalar(
                out=stat[:, c0 + 2:c0 + 3], in0=stat[:, c0:c0 + 1],
                scalar1=stat[:, c0 + 1:c0 + 2],
                scalar2=-1.0, op0=OP.mult, op1=OP.mult))
            mkv = p.mark("vector", "v")
            p.wait("scalar", "v", mkv)
            p.do("scalar", lambda e, xa=xa, tt=tt, c0=c0: e.activation(
                out=ln_tm[:, tt * D:(tt + 1) * D], in_=xa, func=AF.Identity,
                scale=stat[:, c0 + 1:c0 + 2], bias=stat[:, c0 + 2:c0 + 3]))
            return ("a", p.mark("scalar", "a"))

        def ln_stats_apply(slot):
            p.wait("scalar", "v", p.c["v"])
            p.wait("vector", "v", p.c["v"])
            return [ln_tile(slot, tt) for tt in range(TT)]

        def transpose_to_featmaj(src_tm, dst_h, dst_l, dst_f32=None, tt_marks=None):
            vmarks = []
            tbanks = (ps_tr, ps_sc, ps_o, ps_s)
            nset = 0
            for tt in range(TT):
                if tt_marks is not None:
                    p.wait("tensor", tt_marks[tt][0], tt_marks[tt][1])
                for chg in range(2):
                    pb = tbanks[nset % 4]
                    if nset >= 4:
                        p.wait("tensor", "v", vmarks[nset - 4])
                    for cc in range(4):
                        ch = chg * 4 + cc
                        p.do("tensor", lambda e, ch=ch, tt=tt, cc=cc, pb=pb: e.matmul(
                            pb[:, cc * 128:(cc + 1) * 128],
                            src_tm[:, tt * D + ch * 128: tt * D + (ch + 1) * 128],
                            i128[:], start=(cc == 0), stop=(cc == 3)),
                            inc="t" if cc == 3 else None)
                    tm = p.c["t"]
                    p.wait("scalar", "t", tm)
                    pbv = pb[:, 0:512].rearrange("p (c f) -> p c f", f=128)
                    dh = dst_h[:].rearrange("p (c f) -> p c f", f=G)[:, chg * 4:chg * 4 + 4, tt * 128:(tt + 1) * 128]
                    p.do("scalar", lambda e, pbv=pbv, dh=dh: e.copy(out=dh, in_=pbv))
                    if dst_f32 is not None:
                        df = dst_f32[:].rearrange("p (c f) -> p c f", f=G)[:, chg * 4:chg * 4 + 4, tt * 128:(tt + 1) * 128]
                        p.do("scalar", lambda e, pbv=pbv, df=df: e.copy(out=df, in_=pbv))
                    am = p.mark("scalar", "a")
                    p.wait("vector", "a", am)
                    dl = dst_l[:].rearrange("p (c f) -> p c f", f=G)[:, chg * 4:chg * 4 + 4, tt * 128:(tt + 1) * 128]
                    dh2 = dst_h[:].rearrange("p (c f) -> p c f", f=G)[:, chg * 4:chg * 4 + 4, tt * 128:(tt + 1) * 128]
                    p.do("vector", lambda e, pbv=pbv, dl=dl, dh2=dh2: e.tensor_tensor(
                        out=dl, in0=pbv, in1=dh2, op=OP.subtract))
                    vmarks.append(p.mark("vector", "v"))
                    nset += 1
            p.wait("tensor", "v", p.c["v"])
            p.wait("tensor", "a", p.c["a"])

        def mm_tokmaj(Lh, Ll, k0, evac_fn):
            marks = []
            idx = 0
            for half in range(2):
                wh, wl = w_use_begin(k0 + half)
                for tt in range(TT):
                    bank = (idx % 4) * 512
                    if idx >= 4:
                        p.wait("tensor", marks[idx - 4][0], marks[idx - 4][1])
                    first = True
                    for A, W in ((Lh, wh), (Ll, wh), (Lh, wl)):
                        for ch in range(8):
                            last = (A is Lh and W is wl and ch == 7)
                            p.do("tensor", lambda e, ch=ch, tt=tt, A=A, W=W, bank=bank,
                                 first=first, last=last: e.matmul(
                                ps_mm[:, bank:bank + 512],
                                A[:, ch * G + tt * 128: ch * G + (tt + 1) * 128],
                                W[:, ch * 512:(ch + 1) * 512],
                                start=first, stop=last),
                                inc="t" if last else None)
                            first = False
                    marks.append(evac_fn(tt, half, ps_mm[:, bank:bank + 512]))
                    idx += 1
                w_use_end(k0 + half)
            p.wait("tensor", "v", p.c["v"])
            p.wait("tensor", "a", p.c["a"])
            return k0 + 2

        def mm_featmaj(Lh, Ll, k0, evac_fn):
            marks = []
            idx = 0
            for half in range(2):
                wh, wl = w_use_begin(k0 + half)
                for oc in range(4):
                    och = half * 4 + oc
                    bank = (idx % 4) * 512
                    if idx >= 4:
                        p.wait("tensor", marks[idx - 4][0], marks[idx - 4][1])
                    first = True
                    for W, A in ((wh, Lh), (wh, Ll), (wl, Lh)):
                        for ch in range(8):
                            last = (W is wl and ch == 7)
                            p.do("tensor", lambda e, ch=ch, oc=oc, W=W, A=A, bank=bank,
                                 first=first, last=last: e.matmul(
                                ps_mm[:, bank:bank + G],
                                W[:, ch * 512 + oc * 128: ch * 512 + (oc + 1) * 128],
                                A[:, ch * G:(ch + 1) * G],
                                start=first, stop=last),
                                inc="t" if last else None)
                            first = False
                    marks.append(evac_fn(och, ps_mm[:, bank:bank + G]))
                    idx += 1
                w_use_end(k0 + half)
            p.wait("tensor", "v", p.c["v"])
            p.wait("tensor", "a", p.c["a"])
            return k0 + 2

        def softmax_gate(lnF_src):
            zbanks = (ps_tr, ps_sc, ps_o, ps_s)
            zmarks = []
            for tt in range(TT):
                zb = zbanks[tt % 4]
                for ch in range(8):
                    p.do("tensor", lambda e, ch=ch, tt=tt, zb=zb: e.matmul(
                        zb[:, 0:E],
                        lnF_src[:, ch * G + tt * 128: ch * G + (tt + 1) * 128],
                        gw_sb[:, ch * E:(ch + 1) * E],
                        start=(ch == 0), stop=(ch == 7)),
                        inc="t" if ch == 7 else None)
                zmarks.append(p.c["t"])
            sm_marks = []
            for tt in range(TT):
                zb = zbanks[tt % 4]
                p.wait("vector", "t", zmarks[tt])
                p.do("vector", lambda e, zb=zb: e.reduce_max(
                    out=zscr[:, 0:1], in_=zb[:, 0:E], axis=AX.X))
                p.drain("vector")
                p.do("vector", lambda e: e.tensor_scalar(
                    out=zscr[:, 0:1], in0=zscr[:, 0:1], scalar1=-1.0, scalar2=None, op0=OP.mult))
                mk = p.mark("vector", "v")
                p.wait("scalar", "v", mk)
                p.do("scalar", lambda e, zb=zb: e.activation(
                    out=zscr[:, 8:16], in_=zb[:, 0:E], func=AF.Exp,
                    bias=zscr[:, 0:1], scale=1.0))
                am = p.mark("scalar", "a")
                p.wait("vector", "a", am)
                p.do("vector", lambda e: e.reduce_sum(
                    out=zscr[:, 1:2], in_=zscr[:, 8:16], axis=AX.X))
                p.drain("vector")
                p.do("vector", lambda e: e.reciprocal(out=zscr[:, 1:2], in_=zscr[:, 1:2]))
                p.drain("vector")
                p.do("vector", lambda e: e.tensor_scalar(
                    out=zscr[:, 16:24], in0=zscr[:, 8:16], scalar1=zscr[:, 1:2],
                    scalar2=None, op0=OP.mult))
                p.drain("vector")
                p.do("vector", lambda e: e.tensor_scalar(
                    out=zscr[:, 24:32], in0=zscr[:, 16:24], scalar1=TOP_P,
                    scalar2=None, op0=OP.is_gt))
                p.drain("vector")
                p.do("vector", lambda e: e.reduce_sum(
                    out=zscr[:, 2:3], in_=zscr[:, 24:32], axis=AX.X))
                p.drain("vector")
                p.do("vector", lambda e: e.tensor_scalar(
                    out=zscr[:, 3:4], in0=zscr[:, 2:3], scalar1=-1.0, scalar2=float(E),
                    op0=OP.mult, op1=OP.add))
                p.drain("vector")
                p.do("vector", lambda e, tt=tt: e.tensor_tensor(
                    out=s0s1[:, tt:tt + 1], in0=zscr[:, 3:4], in1=zscr[:, 16:17], op=OP.mult))
                p.do("vector", lambda e, tt=tt: e.tensor_tensor(
                    out=s0s1[:, TT + tt:TT + tt + 1], in0=zscr[:, 2:3], in1=zscr[:, 17:18],
                    op=OP.mult))
                sm_marks.append(p.mark("vector", "v"))
            p.wait("vector", "v", sm_marks[-1])
            return p.c["a"]

        def tables_from_m():
            for tt in range(TT):
                p.do("tensor", lambda e, tt=tt: e.matmul(
                    ps_tr[:, tt * H:(tt + 1) * H],
                    m_gr[:, tt * 128:(tt + 1) * 128],
                    i128[0:H, 0:H], start=(tt == 0), stop=(tt == TT - 1)),
                    inc="t" if tt == TT - 1 else None)
            p.wait("vector", "t", p.c["t"])
            p.do("vector", lambda e: e.tensor_copy(out=m_tm[:, 0:TT * H], in_=ps_tr[:, 0:TT * H]))
            mk = p.mark("vector", "v")
            p.wait("tensor", "v", mk)
            p.wait("scalar", "v", mk)
            p.do("scalar", lambda e: e.activation(
                out=t_alpha[:, 0:TT * H], in_=m_tm[:, 0:TT * H], func=AF.Exp, scale=-1.0))
            p.do("scalar", lambda e: e.activation(
                out=t_inva[:, 0:TT * H], in_=m_tm[:, 0:TT * H], func=AF.Exp, scale=1.0))
            p.mark("scalar", "a")
            for tt in range(TT):
                p.do("vector", lambda e, tt=tt: e.tensor_scalar(
                    out=mc_rhs[:, tt * H:(tt + 1) * H], in0=i128[0:H, 0:H],
                    scalar1=m_gr[:, tt * 128 + 127: tt * 128 + 128],
                    scalar2=None, op0=OP.mult))
            mk = p.mark("vector", "v")
            p.wait("tensor", "v", mk)
            p.do("tensor", lambda e: e.matmul(
                ps_tr[:, 0:TT * H], ones16[:], mc_rhs[:, 0:TT * H],
                start=True, stop=True), inc="t")
            tm = p.c["t"]
            p.wait("vector", "t", tm)
            p.wait("scalar", "t", tm)
            p.do("vector", lambda e: e.tensor_tensor(
                out=t_beta[:, 0:TT * H], in0=m_tm[:, 0:TT * H], in1=ps_tr[:, 0:TT * H],
                op=OP.subtract))
            mk = p.mark("vector", "v")
            p.wait("scalar", "v", mk)
            p.do("scalar", lambda e: e.activation(
                out=t_beta[:, 0:TT * H], in_=t_beta[:, 0:TT * H], func=AF.Exp, scale=1.0))
            p.do("scalar", lambda e: e.activation(
                out=t_lam[:, 0:TT * H], in_=ps_tr[:, 0:TT * H], func=AF.Exp, scale=-1.0))
            am = p.mark("scalar", "a")
            p.wait("tensor", "a", am)
            p.wait("vector", "a", am)

        def decay_stage(g, lnF_src, spill=True):
            for ch in range(8):
                p.do("tensor", lambda e, ch=ch: e.matmul(
                    ps_tr[0:H, 0:G],
                    wd_sb[:, ch * H:(ch + 1) * H],
                    lnF_src[:, ch * G:(ch + 1) * G],
                    start=(ch == 0), stop=(ch == 7)),
                    inc="t" if ch == 7 else None)
            tm = p.c["t"]
            p.wait("scalar", "t", tm)
            r_ap = sp_scr[:, 0:G]
            a_ap = sp_scr[:, G:2 * G]
            l_ap = sp_scr[:, G:2 * G]
            m_raw = sp_scr[:, 0:G]
            p.do("scalar", lambda e: e.activation(
                out=r_ap, in_=ps_tr[0:H, 0:G], func=AF.Relu, bias=bd_sb[:], scale=1.0))
            p.do("scalar", lambda e: e.activation(
                out=a_ap, in_=ps_tr[0:H, 0:G], func=AF.Abs, bias=bd_sb[:], scale=1.0))
            p.drain("scalar")
            p.do("scalar", lambda e: e.activation(
                out=a_ap, in_=a_ap, func=AF.Exp, scale=-1.0))
            p.drain("scalar")
            p.do("scalar", lambda e: e.activation(
                out=l_ap, in_=a_ap, func=AF.Ln, bias=csb[0:H, 2:3], scale=1.0))
            am = p.mark("scalar", "a")
            p.wait("vector", "a", am)
            p.wait("tensor", "a", am)
            p.do("vector", lambda e: e.tensor_tensor(
                out=m_raw, in0=r_ap, in1=l_ap, op=OP.add))
            p.drain("vector")
            for tt in range(TT):
                p.do("vector", lambda e, tt=tt: e.tensor_tensor_scan(
                    out=m_gr[:, tt * 128:(tt + 1) * 128],
                    data0=sp_scr[:, tt * 128:(tt + 1) * 128],
                    data1=sp_scr[:, tt * 128:(tt + 1) * 128],
                    initial=0.0, op0=OP.add, op1=OP.bypass))
            mk = p.mark("vector", "v")
            if spill:
                p.wait("gpsimd", "v", mk)
                dma("gpsimd", sp_m[:, g * G:(g + 1) * G], m_gr[:], "qg")
            p.wait("tensor", "v", mk)

        def gla_kgb_batch(tt, buf):
            p.wait("scalar", "t", p.c["t"])
            p.wait("scalar", "a", p.c["a"])
            p.wait("scalar", "qg", p.c["qg"])
            for h in range(H):
                col = tt * H + h
                p.do("scalar", lambda e, tt=tt, h=h, col=col, buf=buf: e.activation(
                    out=buf[:, h * DH:(h + 1) * DH],
                    in_=b_kgT[:, tt * D + h * DH: tt * D + (h + 1) * DH],
                    func=AF.Copy, scale=t_beta[:, col:col + 1]))
            amk = p.mark("scalar", "a")
            return (amk, p.c["v"])

        def gla_state_chunk(gchunk, tt, g=None, pre=None):
            cur = s_bufs[gchunk % 2]
            nxt = s_bufs[(gchunk + 1) % 2]
            amk, buf = pre
            if g is not None:
                p.wait("gpsimd", "a", amk)
                rows = sp_kgT[g * G + tt * 128: g * G + (tt + 1) * 128, :]
                dma("gpsimd", rows, buf[:], "qg")
            nxt_pre = None
            if tt + 1 < TT:
                a2, _ = gla_kgb_batch(tt + 1, kgb_bufs[(tt + 1) % 2])
                nxt_pre = (a2, kgb_bufs[(tt + 1) % 2])
            kgb_all_local = buf
            p.wait("tensor", "a", amk)
            sd_banks = (ps_sc, ps_s, ps_tr, ps_o)
            ev = []
            for hp in range(H // 2):
                h = 2 * hp
                sdb = sd_banks[hp % 4]
                if hp >= 4:
                    p.wait("tensor", "v", ev[hp - 4])
                p.do("tensor", lambda e, tt=tt, h=h, sdb=sdb, buf=kgb_all_local: e.matmul(
                    sdb[:, 0:128], buf[:, h * DH:(h + 2) * DH],
                    b_v[:, tt * D + h * DH: tt * D + (h + 2) * DH],
                    start=True, stop=True), inc="t")
                tm = p.c["t"]
                p.wait("vector", "t", tm)
                for i in (0, 1):
                    hh = h + i
                    hb = (hh % 2) * 64
                    hc = (hh // 2) * 64
                    col = tt * H + hh
                    p.do("vector", lambda e, cur=cur, nxt=nxt, hb=hb, hc=hc, col=col,
                         sdb=sdb, i=i: e.scalar_tensor_tensor(
                             out=nxt[hb:hb + 64, hc:hc + 64],
                             in0=cur[hb:hb + 64, hc:hc + 64],
                             scalar=t_lam[hb:hb + 64, col:col + 1],
                             in1=sdb[i * 64:(i + 1) * 64, i * 64:(i + 1) * 64],
                             op0=OP.mult, op1=OP.add))
                ev.append(p.mark("vector", "v"))
            p.wait("tensor", "v", p.c["v"])
            return nxt_pre

        def gla_out_chunk(gchunk, tt, qF_src, kgF_src):
            cur = s_bufs[gchunk % 2]
            nxt = s_bufs[(gchunk + 1) % 2]

            def qk(h):
                hb = (h % 2) * 64
                c0 = (h // 2) * G + tt * 128
                return (qF_src[hb:hb + 64, c0:c0 + 128],
                        kgF_src[hb:hb + 64, c0:c0 + 128])

            m_st_pre = [None] * H

            sc_marks = [None] * H

            def emit_score(h):
                b = h % 4
                qsl, ksl = qk(h)
                if h >= 4:
                    p.wait("tensor", "v", m_st_pre[h - 4])
                p.do("tensor", lambda e, b=b, ksl=ksl, qsl=qsl: e.matmul(
                    ps_mm[:, b * 512: b * 512 + 128],
                    ksl, qsl, start=True, stop=True), inc="t")
                sc_marks[h] = p.c["t"]

            for h in range(4):
                emit_score(h)
            m_intra = [None] * H
            m_oe = [None] * H
            ob_banks = (ps_sc, ps_s)
            sd_banks = (ps_tr, ps_o)
            sdp_marks = []
            for h in range(H):
                b = h % 4
                qsl, ksl = qk(h)
                hb = (h % 2) * 64
                hc = (h // 2) * 64
                col = tt * H + h
                stb = St4[:, b * 128:(b + 1) * 128]
                obb = ob_banks[h % 2]
                p.wait("vector", "t", sc_marks[h])
                if h >= 4:
                    p.wait("vector", "t", m_intra[h - 4])
                p.do("vector", lambda e, col=col, b=b, stb=stb: e.scalar_tensor_tensor(
                    out=stb, in0=ps_mm[:, b * 512: b * 512 + 128],
                    scalar=t_inva[:, col:col + 1],
                    in1=tri[:], op0=OP.mult, op1=OP.mult))
                m_st_pre[h] = p.mark("vector", "v")
                if h + 4 < H:
                    emit_score(h + 4)
                if h == 0 and wo_state["mark"] is not None:
                    p.wait("tensor", "v", wo_state["mark"])
                if h >= 2:
                    p.wait("tensor", "a", m_oe[h - 2])
                p.do("tensor", lambda e, qsl=qsl, cur=cur, hb=hb, hc=hc, obb=obb: e.matmul(
                    obb[:, 0:DH], qsl, cur[hb:hb + 64, hc:hc + 64],
                    start=True, stop=False))
                p.wait("tensor", "v", m_st_pre[h])
                p.do("tensor", lambda e, tt=tt, h=h, stb=stb, obb=obb: e.matmul(
                    obb[:, 0:DH], stb,
                    b_v[:, tt * D + h * DH: tt * D + (h + 1) * DH],
                    start=False, stop=True), inc="t")
                m_intra[h] = p.c["t"]
                p.wait("scalar", "t", m_intra[h])
                p.do("scalar", lambda e, h=h, col=col, obb=obb: e.activation(
                    out=o_tm[:, h * DH:(h + 1) * DH], in_=obb[:, 0:DH],
                    func=AF.Copy, scale=t_alpha[:, col:col + 1]))
                m_oe[h] = p.mark("scalar", "a")
                if h % 2 == 1:
                    hp = h // 2
                    sdb = sd_banks[hp % 2]
                    if hp >= 2:
                        p.wait("tensor", "v", sdp_marks[hp - 2])
                    p.do("tensor", lambda e, tt=tt, h=h, sdb=sdb: e.matmul(
                        sdb[:, 0:128], b_kgT[:, tt * D + (h - 1) * DH: tt * D + (h + 1) * DH],
                        b_v[:, tt * D + (h - 1) * DH: tt * D + (h + 1) * DH],
                        start=True, stop=True), inc="t")
                    tm = p.c["t"]
                    p.wait("vector", "t", tm)
                    for i in (0, 1):
                        hh = h - 1 + i
                        hhb = (hh % 2) * 64
                        hhc = (hh // 2) * 64
                        ccol = tt * H + hh
                        p.do("vector", lambda e, cur=cur, nxt=nxt, hhb=hhb, hhc=hhc,
                             ccol=ccol, sdb=sdb, i=i: e.scalar_tensor_tensor(
                                 out=nxt[hhb:hhb + 64, hhc:hhc + 64],
                                 in0=cur[hhb:hhb + 64, hhc:hhc + 64],
                                 scalar=t_lam[hhb:hhb + 64, ccol:ccol + 1],
                                 in1=sdb[i * 64:(i + 1) * 64, i * 64:(i + 1) * 64],
                                 op0=OP.mult, op1=OP.add))
                    sdp_marks.append(p.mark("vector", "v"))
            p.wait("tensor", "v", p.c["v"])
            p.wait("scalar", "v", p.c["v"])

        def phase_a_group(g):
            slot = g % 2
            qg_prev = p.c["qg"]
            p.wait("vector", "qg", qg_prev)
            p.wait("scalar", "qg", qg_prev)
            if g + 1 < NG:
                p.wait("gpsimd", "v", p.c["v"])
                p.wait("gpsimd", "a", p.c["a"])
                sem = "qx0" if (g + 1) % 2 == 0 else "qx1"
                dma("gpsimd", xgrp((g + 1) % 2).rearrange("p (t c) -> p t c", c=D),
                    tokmaj_dram_view(x_in, g + 1), sem)
                xwm[g + 1] = (sem, p.c[sem])
            if g > 0:
                p.wait("vector", xwm[g][0], xwm[g][1])
                p.wait("scalar", xwm[g][0], xwm[g][1])
            lnm = ln_stats_apply(slot)
            transpose_to_featmaj(ln_tm, b_lnFh, b_lnFl, b_lnF, tt_marks=lnm)
            softmax_gate(b_lnF)
            k0 = ws["used"]

            def evac_e0(tt, half, ps):
                p.wait("scalar", "t", p.c["t"])
                p.wait("scalar", "v", p.c["v"])
                p.do("scalar", lambda e, tt=tt, half=half, ps=ps: e.activation(
                    out=b_f[:, tt * D + half * 512: tt * D + half * 512 + 512],
                    in_=ps, func=AF.Identity, scale=s0s1[:, tt:tt + 1]))
                return ("a", p.mark("scalar", "a"))
            k0 = mm_tokmaj(b_lnFh, b_lnFl, k0, evac_e0)

            def evac_e1(tt, half, ps):
                p.wait("vector", "t", p.c["t"])
                p.do("vector", lambda e, tt=tt, half=half, ps=ps: e.scalar_tensor_tensor(
                    out=b_f[:, tt * D + half * 512: tt * D + half * 512 + 512],
                    in0=ps, scalar=s0s1[:, TT + tt:TT + tt + 1],
                    in1=b_f[:, tt * D + half * 512: tt * D + half * 512 + 512],
                    op0=OP.mult, op1=OP.add))
                return ("v", p.mark("vector", "v"))
            k0 = mm_tokmaj(b_lnFh, b_lnFl, k0, evac_e1)

            transpose_to_featmaj(b_f, b_lnFh, b_lnFl)

            def evac_out(tt, half, ps):
                p.wait("vector", "t", p.c["t"])
                dst = xd[:, slot * TT * D + tt * D + half * 512:
                         slot * TT * D + tt * D + half * 512 + 512]
                p.do("vector", lambda e, ps=ps, dst=dst: e.scalar_tensor_tensor(
                    out=dst, in0=ps, scalar=0.5, in1=dst, op0=OP.mult, op1=OP.add))
                return ("v", p.mark("vector", "v"))
            k0 = mm_tokmaj(b_lnFh, b_lnFl, k0, evac_out)

            p.wait("gpsimd", "v", p.c["v"])
            dma("gpsimd", tokmaj_dram_view(sp_x1, g),
                xgrp(slot).rearrange("p (t c) -> p t c", c=D), "qg")
            if cfg.stop_after == "ffn1":
                dma("gpsimd", tokmaj_dram_view(y_out, g), xgrp(slot).rearrange(
                    "p (t c) -> p t c", c=D), "qg")
                return

            lnm = ln_stats_apply(slot)
            transpose_to_featmaj(ln_tm, b_lnFh, b_lnFl, b_lnF, tt_marks=lnm)
            p.wait("gpsimd", "v", p.c["v"])
            p.wait("gpsimd", "a", p.c["a"])
            dma("gpsimd", featmaj_dram_view(sp_lnFh, g), featmaj_sb_view(b_lnFh), "qg")
            dma("gpsimd", featmaj_dram_view(sp_lnFl, g), featmaj_sb_view(b_lnFl), "qg")

            def evac_g(och, ps):
                p.wait("scalar", "t", p.c["t"])
                p.do("scalar", lambda e, och=och, ps=ps: e.activation(
                    out=b_aux[:, och * G:(och + 1) * G], in_=ps, func=AF.Sigmoid))
                return ("a", p.mark("scalar", "a"))
            k0 = mm_featmaj(b_lnFh, b_lnFl, k0, evac_g)

            def evac_k(och, ps):
                p.wait("vector", "t", p.c["t"])
                p.wait("vector", "a", p.c["a"])
                p.do("vector", lambda e, och=och, ps=ps: e.tensor_tensor(
                    out=b_kgF[:, och * G:(och + 1) * G], in0=ps,
                    in1=b_aux[:, och * G:(och + 1) * G], op=OP.mult))
                return ("v", p.mark("vector", "v"))
            k0 = mm_featmaj(b_lnFh, b_lnFl, k0, evac_k)

            def evac_v(tt, half, ps):
                p.wait("scalar", "t", p.c["t"])
                p.do("scalar", lambda e, tt=tt, half=half, ps=ps: e.copy(
                    out=b_v[:, tt * D + half * 512: tt * D + half * 512 + 512], in_=ps))
                return ("a", p.mark("scalar", "a"))
            k0 = mm_tokmaj(b_lnFh, b_lnFl, k0, evac_v)

            p.wait("gpsimd", "v", p.c["v"])
            p.wait("gpsimd", "a", p.c["a"])
            dma("gpsimd", featmaj_dram_view(sp_kgF, g), featmaj_sb_view(b_kgF), "qg")
            dma("gpsimd", tokmaj_dram_view(sp_v, g), tokmaj_sb_view(b_v), "qg")

            decay_stage(g, b_lnF, spill=True)
            vmarks2 = []
            tbanks4 = (ps_tr, ps_sc, ps_o, ps_s)
            seti = 0
            for tt in range(TT):
                for ch in range(8):
                    pb = tbanks4[seti % 4]
                    p.do("tensor", lambda e, tt=tt, ch=ch, pb=pb: e.matmul(
                        pb[:, (ch % 4) * 128:(ch % 4) * 128 + 128],
                        b_kgF[:, ch * G + tt * 128: ch * G + (tt + 1) * 128],
                        i128[:], start=(ch % 4 == 0), stop=(ch % 4 == 3)),
                        inc="t" if ch % 4 == 3 else None)
                    if ch % 4 == 3:
                        p.wait("scalar", "t", p.c["t"])
                        base = (ch // 4) * 512
                        p.do("scalar", lambda e, tt=tt, base=base, pb=pb: e.copy(
                            out=b_kgT[:, tt * D + base: tt * D + base + 512],
                            in_=pb[:, 0:512]))
                        vmarks2.append(("a", p.mark("scalar", "a")))
                        seti += 1
                        if seti >= 4:
                            p.wait("tensor", vmarks2[seti - 4][0], vmarks2[seti - 4][1])
            p.wait("tensor", "v", p.c["v"])
            tables_from_m()
            if cfg.stop_after == "proj":
                return
            pre = gla_kgb_batch(0, kgb_bufs[0])
            pre = (pre[0], kgb_bufs[0])
            for tt in range(TT):
                pre = gla_state_chunk(g * TT + tt, tt, g=g, pre=pre)

        bwm = {}
        relay_wm = {}
        xwm = {}
        wo_state = {"mark": None}

        def issue_b_m(g):
            p.wait("gpsimd", "t", p.c["t"])
            p.wait("gpsimd", "v", p.c["v"])
            p.wait("gpsimd", "qg", p.c["qg"])
            dma("gpsimd", m_gr[:], sp_m[:, g * G:(g + 1) * G], "qs")
            bwm[(g, "m")] = p.c["qs"]

        def issue_b_big(g):
            slot = g % 2
            p.wait("gpsimd", "t", p.c["t"])
            p.wait("gpsimd", "v", p.c["v"])
            p.wait("gpsimd", "a", p.c["a"])
            p.wait("gpsimd", "qg", p.c["qg"])
            dma("gpsimd", xgrp(slot).rearrange("p (t c) -> p t c", c=D),
                tokmaj_dram_view(sp_x1, g), "qs2")
            dma("gpsimd", featmaj_sb_view(b_kgF), featmaj_dram_view(sp_kgF, g), "qs2")
            dma("gpsimd", tokmaj_sb_view(b_kgT), tokmaj_dram_view(sp_kgT, g), "qs2")
            dma("gpsimd", tokmaj_sb_view(b_v), tokmaj_dram_view(sp_v, g), "qs2")
            bwm[(g, "late")] = p.c["qs2"]

        def issue_b_lnf(g):
            p.wait("gpsimd", "t", p.c["t"])
            p.wait("gpsimd", "qg", p.c["qg"])
            dma("gpsimd", featmaj_sb_view(b_lnFh), featmaj_dram_view(sp_lnFh, g), "qs3")
            dma("gpsimd", featmaj_sb_view(b_lnFl), featmaj_dram_view(sp_lnFl, g), "qs3")
            bwm[(g, "lnf")] = p.c["qs3"]

        def issue_b_loads(g):
            issue_b_m(g)
            issue_b_big(g)
            issue_b_lnf(g)

        def phase_b_group(g):
            slot = g % 2
            qload = bwm[(g, "late")]
            for e in ("tensor", "vector", "scalar"):
                p.wait(e, "qs", bwm[(g, "m")])
            tables_from_m()
            if g + 1 < NG:
                issue_b_m(g + 1)
            for e in ("tensor", "vector", "scalar"):
                p.wait(e, "qs3", bwm[(g, "lnf")])
            k0 = ws["used"]

            def evac_q(och, ps):
                p.wait("scalar", "t", p.c["t"])
                p.do("scalar", lambda e, och=och, ps=ps: e.copy(
                    out=b_aux[:, och * G:(och + 1) * G], in_=ps))
                return ("a", p.mark("scalar", "a"))
            k0 = mm_featmaj(b_lnFh, b_lnFl, k0, evac_q)

            def evac_og(tt, half, ps):
                p.wait("scalar", "t", p.c["t"])
                p.do("scalar", lambda e, tt=tt, half=half, ps=ps: e.activation(
                    out=b_og[:, tt * D + half * 512: tt * D + half * 512 + 512],
                    in_=ps, func=AF.Sigmoid))
                return ("a", p.mark("scalar", "a"))
            k0 = mm_tokmaj(b_lnFh, b_lnFl, k0, evac_og)

            if g == 0:
                p.wait("vector", "qg", relay_wm["qg"])
                p.do("vector", lambda e: e.tensor_scalar(
                    out=s_bufs[0][:], in0=relay_wm["oth"][:], scalar1=umask[:],
                    scalar2=None, op0=OP.mult))
                p.mark("vector", "v")
            for e in ("tensor", "vector", "scalar"):
                p.wait(e, "qs2", qload)
            wo_h = (w_use_begin(k0), w_use_begin(k0 + 1))
            lnm = []
            wo_state["mark"] = None

            def emit_wo_tile(tt):
                for half in range(2):
                    wh, wl = wo_h[half]
                    wob = ps_sc if half == 0 else ps_s
                    first = True
                    for A, W in ((b_lnFh, wh), (b_lnFl, wh), (b_lnFh, wl)):
                        for ch in range(8):
                            last = (A is b_lnFh and W is wl and ch == 7)
                            p.do("tensor", lambda e, ch=ch, tt=tt, A=A, W=W, wob=wob,
                                 first=first, last=last: e.matmul(
                                wob[:, 0:512],
                                A[:, ch * G + tt * 128: ch * G + (tt + 1) * 128],
                                W[:, ch * 512:(ch + 1) * 512],
                                start=first, stop=last),
                                inc="t" if last else None)
                            first = False
                    p.wait("vector", "t", p.c["t"])
                    dst = xd[:, slot * TT * D + tt * D + half * 512:
                             slot * TT * D + tt * D + half * 512 + 512]
                    p.do("vector", lambda e, dst=dst, wob=wob: e.tensor_tensor(
                        out=dst, in0=wob[:, 0:512], in1=dst, op=OP.add))
                    wo_state["mark"] = p.mark("vector", "v")

            for tt in range(TT):
                gla_out_chunk(g * TT + tt, tt, b_aux, b_kgF)
                p.wait("scalar", "v", p.c["v"])
                p.wait("scalar", "a", p.c["a"])
                p.do("scalar", lambda e: e.activation(
                    out=kgb_all[:], in_=o_tm[:], func=AF.Square, accum_out=stat[:, 0:1]))
                p.drain("scalar")
                p.do("scalar", lambda e: e.activation(
                    out=stat[:, 0:1], in_=stat[:, 0:1], func=AF.Sqrt, bias=csb[:, 1:2],
                    scale=1.0 / D))
                am = p.mark("scalar", "a")
                p.wait("vector", "a", am)
                p.do("vector", lambda e: e.reciprocal(out=stat[:, 0:1], in_=stat[:, 0:1]))
                p.drain("vector")
                p.do("vector", lambda e: e.scalar_tensor_tensor(
                    out=o_tm[:], in0=o_tm[:], scalar=stat[:, 0:1], in1=nwb[:],
                    op0=OP.mult, op1=OP.mult))
                p.drain("vector")
                p.do("vector", lambda e, tt=tt: e.tensor_tensor(
                    out=o_tm[:], in0=o_tm[:], in1=b_og[:, tt * D:(tt + 1) * D], op=OP.mult))
                mk = p.mark("vector", "v")
                p.wait("tensor", "v", mk)
                for ch in range(8):
                    p.do("tensor", lambda e, ch=ch: e.matmul(
                        ps_tr[:, (ch % 4) * 128:(ch % 4) * 128 + 128],
                        o_tm[:, ch * 128:(ch + 1) * 128],
                        i128[:], start=(ch % 4 == 0), stop=(ch % 4 == 3)),
                        inc="t" if ch % 4 == 3 else None)
                    if ch % 4 == 3:
                        tmn = p.c["t"]
                        p.wait("scalar", "t", tmn)
                        for cc2 in range(4):
                            ch2 = (ch // 4) * 4 + cc2
                            p.do("scalar", lambda e, ch2=ch2, cc2=cc2, tt=tt: e.copy(
                                out=b_lnFh[:, ch2 * G + tt * 128: ch2 * G + (tt + 1) * 128],
                                in_=ps_tr[:, cc2 * 128:(cc2 + 1) * 128]))
                        am = p.mark("scalar", "a")
                        p.wait("vector", "a", am)
                        for cc2 in range(4):
                            ch2 = (ch // 4) * 4 + cc2
                            p.do("vector", lambda e, ch2=ch2, cc2=cc2, tt=tt: e.tensor_tensor(
                                out=b_lnFl[:, ch2 * G + tt * 128: ch2 * G + (tt + 1) * 128],
                                in0=ps_tr[:, cc2 * 128:(cc2 + 1) * 128],
                                in1=b_lnFh[:, ch2 * G + tt * 128: ch2 * G + (tt + 1) * 128],
                                op=OP.subtract))
                        mk = p.mark("vector", "v")
                        p.wait("tensor", "v", mk)
                emit_wo_tile(tt)
                lnm.append(ln_tile(slot, tt))
            w_use_end(k0)
            w_use_end(k0 + 1)
            k0 = k0 + 2
            if g + 1 < NG:
                issue_b_big(g + 1)

            if wo_state["mark"] is not None:
                p.wait("tensor", "v", wo_state["mark"])
            transpose_to_featmaj(ln_tm, b_lnFh, b_lnFl, b_lnF, tt_marks=lnm)
            softmax_gate(b_lnF)

            def evac2_e0(tt, half, ps):
                p.wait("scalar", "t", p.c["t"])
                p.wait("scalar", "v", p.c["v"])
                p.do("scalar", lambda e, tt=tt, half=half, ps=ps: e.activation(
                    out=b_f[:, tt * D + half * 512: tt * D + half * 512 + 512],
                    in_=ps, func=AF.Identity, scale=s0s1[:, tt:tt + 1]))
                return ("a", p.mark("scalar", "a"))
            k0 = mm_tokmaj(b_lnFh, b_lnFl, k0, evac2_e0)

            def evac2_e1(tt, half, ps):
                p.wait("vector", "t", p.c["t"])
                p.do("vector", lambda e, tt=tt, half=half, ps=ps: e.scalar_tensor_tensor(
                    out=b_f[:, tt * D + half * 512: tt * D + half * 512 + 512],
                    in0=ps, scalar=s0s1[:, TT + tt:TT + tt + 1],
                    in1=b_f[:, tt * D + half * 512: tt * D + half * 512 + 512],
                    op0=OP.mult, op1=OP.add))
                return ("v", p.mark("vector", "v"))
            k0 = mm_tokmaj(b_lnFh, b_lnFl, k0, evac2_e1)

            transpose_to_featmaj(b_f, b_lnFh, b_lnFl)

            def evac2_out(tt, half, ps):
                p.wait("vector", "t", p.c["t"])
                dst = xd[:, slot * TT * D + tt * D + half * 512:
                         slot * TT * D + tt * D + half * 512 + 512]
                p.do("vector", lambda e, ps=ps, dst=dst: e.scalar_tensor_tensor(
                    out=dst, in0=ps, scalar=0.5, in1=dst, op0=OP.mult, op1=OP.add))
                return ("v", p.mark("vector", "v"))
            k0 = mm_tokmaj(b_lnFh, b_lnFl, k0, evac2_out)

            p.wait("gpsimd", "v", p.c["v"])
            dma("gpsimd", tokmaj_dram_view(y_out, g), xgrp(slot).rearrange(
                "p (t c) -> p t c", c=D), "qg")
            if g + 1 < NG:
                issue_b_lnf(g + 1)

        def program(pp):
            ws["loaded"] = 0
            ws["used"] = 0
            ws["tfree"] = 0
            dma("gpsimd", i128[:], i128_in[:], "qg")
            dma("gpsimd", tri[:], tri_in[:], "qg")
            dma("gpsimd", nwb[:], nwb_in[:], "qg")
            dma("gpsimd", ones1[:], ones_in[:], "qg")
            dma("gpsimd", ones16[:], ones16_in[:], "qg")
            dma("gpsimd", cmask[:], cmask_in[:], "qg")
            dma("gpsimd", umask[:], umask_in[:], "qg")
            dma("gpsimd", csb[:], consts_in[:], "qg")
            dma("gpsimd", bd_sb[:], bd_in[:], "qg")
            dma("gpsimd", gw_sb[:].rearrange("p (k c) -> p k c", c=E),
                gw[:].rearrange("(k p) c -> p k c", p=128), "qg")
            dma("gpsimd", wd_sb[:].rearrange("p (k c) -> p k c", c=H),
                wd[:].rearrange("(k p) c -> p k c", p=128), "qg")
            dma("gpsimd", xgrp(0).rearrange("p (t c) -> p t c", c=D),
                tokmaj_dram_view(x_in, 0), "qx0")
            xwm[0] = ("qx0", p.c["qx0"])
            cb = p.c["qg"]
            for e in ("vector", "scalar", "tensor", "gpsimd"):
                p.wait(e, "qg", cb)
            p.wait("vector", xwm[0][0], xwm[0][1])
            p.wait("scalar", xwm[0][0], xwm[0][1])
            p.do("vector", lambda e: e.memset(s_bufs[0][:], 0.0))
            p.mark("vector", "v")
            weight_prologue()
            ws["qw0_base"] = p.c["qw0"]
            ws["qw1_base"] = p.c["qw1"]
            w_issue_one()
            w_issue_one()
            for g in range(NG):
                phase_a_group(g)
            if cfg.stop_after in ("ffn1", "proj"):
                p.wait("gpsimd", "qg", p.c["qg"])
                return
            fin = s_bufs[NCH % 2]
            oth = s_bufs[(NCH + 1) % 2]
            p.wait("vector", "v", p.c["v"])
            p.do("vector", lambda e, fin=fin, oth=oth: e.tensor_scalar(
                out=oth[:], in0=fin[:], scalar1=cmask[:], scalar2=None, op0=OP.mult))
            mk = p.mark("vector", "v")
            p.wait("gpsimd", "v", mk)
            dma("gpsimd", cc_in[:], oth[:], "qg")
            issue_b_loads(0)
            p.wait("gpsimd", "qg", p.c["qg"])
            p.do("gpsimd", lambda e: e.collective_compute(
                "AllReduce", OP.add, replica_groups=cfg.pairs,
                ins=[cc_in.opt()], outs=[cc_out.opt()]), inc="cc", n=1)
            p.wait("gpsimd", "cc", p.c["cc"])
            dma("gpsimd", oth[:], cc_out[:], "qg")
            relay_wm["qg"] = p.c["qg"]
            relay_wm["oth"] = oth
            for g in range(NG):
                phase_b_group(g)
            p.wait("gpsimd", "qg", p.c["qg"])
            p.wait("gpsimd", "qs", p.c["qs"])
            p.wait("gpsimd", "qs2", p.c["qs2"])
            p.wait("gpsimd", "qs3", p.c["qs3"])
            p.wait("gpsimd", "qw0", p.c["qw0"])
            p.wait("gpsimd", "qw1", p.c["qw1"])

        @block.sync
        def _(eng):
            p.begin("sync", eng)
            program(p)

        @block.vector
        def _(eng):
            p.begin("vector", eng)
            program(p)

        @block.scalar
        def _(eng):
            p.begin("scalar", eng)
            program(p)

        @block.tensor
        def _(eng):
            p.begin("tensor", eng)
            program(p)

        @block.gpsimd
        def _(eng):
            p.begin("gpsimd", eng)
            program(p)

    return nc


def prep_weights(Wq, Wk, Wv, Wo, Wg, Wog, Wd, bd, norm_w, gate_W, expert_W, out_W):
    T = lambda a: np.ascontiguousarray(np.asarray(a, np.float32).T)
    return {
        "wq": T(Wq), "wk": T(Wk), "wv": T(Wv), "wg": T(Wg), "wog": T(Wog),
        "wo": T(Wo), "e0": T(expert_W[0]), "e1": T(expert_W[1]), "ow": T(out_W),
        "gw": T(gate_W), "wd": T(Wd),
        "bd": np.ascontiguousarray(np.asarray(bd, np.float32).reshape(-1, 1)),
        "normw_b": np.ascontiguousarray(
            np.broadcast_to(np.asarray(norm_w, np.float32), (128, 1024))),
        "i128": np.eye(128, dtype=np.float32),
        "consts": np.ascontiguousarray(np.broadcast_to(
            np.array([LN_EPS, RMS_EPS, 1.0, 0.0], np.float32), (128, 4))),
        "tri": np.triu(np.ones((128, 128), np.float32)),
        "ones1": np.ones((1, 128), np.float32),
        "ones16": np.ones((16, 128), np.float32),
    }


def make_inputs_per_core(cfg, core, x_full, W):
    NT = cfg.ntok
    d = dict(W)
    d["x"] = np.ascontiguousarray(x_full[core * NT:(core + 1) * NT])
    d["cmask"] = np.full((128, 1), 1.0 if core % 2 == 0 else 0.0, np.float32)
    d["umask"] = np.full((128, 1), 1.0 if core % 2 == 1 else 0.0, np.float32)
    return d


def kernel(L0_attn_x, L0_attn_Wq, L0_attn_Wk, L0_attn_Wv, L0_attn_Wo,
           L0_attn_Wg, L0_attn_Wog, L0_attn_Wd, L0_attn_bd, L0_attn_norm_w,
           L0_ffn_gate_W, L0_ffn_expert_W, L0_ffn_out_W):
    cfg = Cfg(n_cores=8, ntok=4096, G=512)
    x = np.ascontiguousarray(np.asarray(L0_attn_x, np.float32).reshape(-1, 1024))
    W = prep_weights(L0_attn_Wq, L0_attn_Wk, L0_attn_Wv, L0_attn_Wo, L0_attn_Wg,
                     L0_attn_Wog, L0_attn_Wd, L0_attn_bd, L0_attn_norm_w,
                     L0_ffn_gate_W, L0_ffn_expert_W, L0_ffn_out_W)
    nc = build(cfg)
    in_maps = [make_inputs_per_core(cfg, c, x, W) for c in range(cfg.n_cores)]
    res = run_bass_kernel_spmd(nc, in_maps, core_ids=list(range(cfg.n_cores)))
    y = np.concatenate([np.asarray(r["y"]) for r in res.results], axis=0)
    return y.reshape(4, 8192, 1024)


# version 27
# speedup vs baseline: 1.0083x; 1.0083x over previous
import numpy as np
import concourse.bass as bass
import concourse.mybir as mybir
from concourse.bass_utils import run_bass_kernel_spmd
from contextlib import ExitStack

F32 = mybir.dt.float32
BF16 = mybir.dt.bfloat16
AF = mybir.ActivationFunctionType
OP = mybir.AluOpType
AX = mybir.AxisListType

LN_EPS = 1e-5
RMS_EPS = float(np.finfo(np.float32).eps)
TOP_P = 0.3


class Cfg:
    def __init__(self, n_cores=8, ntok=4096, G=512, stop_after=None):
        self.n_cores = n_cores
        self.ntok = ntok
        self.G = G
        self.C = 128
        self.H = 16
        self.DH = 64
        self.D = 1024
        self.E = 8
        self.n_groups = ntok // G
        self.TT = G // 128
        self.stop_after = stop_after
        self.pairs = [[i, i + 1] for i in range(0, n_cores, 2)]


class P:

    def __init__(self, nc, sems):
        self.nc = nc
        self.sems = sems
        self.active = None
        self.eng = None
        self.c = None

    def begin(self, active, eng):
        self.active = active
        self.eng = eng
        self.c = {k: 0 for k in self.sems}
        self.last = None

    def inc(self, sem, n=1):
        self.c[sem] += n
        return self.c[sem]

    def do(self, engine, fn, inc=None, n=1):
        if self.active == engine:
            inst = fn(self.eng)
            self.last = inst
            if inc is not None:
                inst.then_inc(self.sems[inc], n)
        if inc is not None:
            return self.inc(inc, n)
        return None

    def mark(self, engine, sem, n=1):
        if self.active == engine:
            self.last.then_inc(self.sems[sem], n)
        return self.inc(sem, n)

    def wait(self, engine, sem, val):
        assert val <= self.c[sem], (sem, val, self.c[sem])
        if self.active == engine:
            self.eng.wait_ge(self.sems[sem], val)

    def wait_future(self, engine, sem, val):
        if self.active == engine:
            self.eng.wait_ge(self.sems[sem], val)

    def drain_inc(self, engine, sem, n=1):
        if self.active == engine:
            self.eng.drain().then_inc(self.sems[sem], n)
        return self.inc(sem, n)

    def drain(self, engine):
        if self.active == engine:
            self.eng.drain()


def build(cfg: Cfg) -> bass.Bass:
    nc = bass.Bass("TRN2", target_bir_lowering=False)
    NT, G, TT, H, DH, D, E = cfg.ntok, cfg.G, cfg.TT, cfg.H, cfg.DH, cfg.D, cfg.E
    NG = cfg.n_groups
    NCH = NT // 128

    x_in = nc.dram_tensor("x", [NT, D], F32, kind="ExternalInput").ap()
    y_out = nc.dram_tensor("y", [NT, D], F32, kind="ExternalOutput").ap()
    wmats = {}
    for nm in ("wq", "wk", "wv", "wg", "wog", "wo", "e0", "e1", "ow"):
        wmats[nm] = nc.dram_tensor(nm, [D, D], F32, kind="ExternalInput").ap()
    gw = nc.dram_tensor("gw", [D, E], F32, kind="ExternalInput").ap()
    wd = nc.dram_tensor("wd", [D, H], F32, kind="ExternalInput").ap()
    bd_in = nc.dram_tensor("bd", [H, 1], F32, kind="ExternalInput").ap()
    nwb_in = nc.dram_tensor("normw_b", [128, D], F32, kind="ExternalInput").ap()
    i128_in = nc.dram_tensor("i128", [128, 128], F32, kind="ExternalInput").ap()
    tri_in = nc.dram_tensor("tri", [128, 128], F32, kind="ExternalInput").ap()
    ones_in = nc.dram_tensor("ones1", [1, 128], F32, kind="ExternalInput").ap()
    ones16_in = nc.dram_tensor("ones16", [16, 128], F32, kind="ExternalInput").ap()
    cmask_in = nc.dram_tensor("cmask", [128, 1], F32, kind="ExternalInput").ap()
    umask_in = nc.dram_tensor("umask", [128, 1], F32, kind="ExternalInput").ap()
    consts_in = nc.dram_tensor("consts", [128, 4], F32, kind="ExternalInput").ap()

    sp_x1 = nc.dram_tensor("sp_x1", [NT, D], F32).ap()
    sp_lnFh = nc.dram_tensor("sp_lnFh", [D, NT], BF16).ap()
    sp_lnFl = nc.dram_tensor("sp_lnFl", [D, NT], BF16).ap()
    whi_d = {}
    wlo_d = {}
    for nm in ("wq", "wk", "wv", "wg", "wog", "wo", "e0", "e1", "ow"):
        whi_d[nm] = nc.dram_tensor("whi_" + nm, [D, D], BF16).ap()
        wlo_d[nm] = nc.dram_tensor("wlo_" + nm, [D, D], BF16).ap()
    sp_kgF = nc.dram_tensor("sp_kgF", [D, NT], F32).ap()
    sp_kgT = nc.dram_tensor("sp_kgT", [NT, D], F32).ap()
    sp_v = nc.dram_tensor("sp_v", [NT, D], F32).ap()
    sp_m = nc.dram_tensor("sp_m", [H, NT], F32).ap()
    cc_in = nc.dram_tensor("cc_in", [128, (H // 2) * DH], F32).ap()
    cc_out = nc.dram_tensor("cc_out", [128, (H // 2) * DH], F32).ap()

    with ExitStack() as ctx:
        sb = lambda name, shape, dt=F32: ctx.enter_context(nc.sbuf_tensor(name, shape, dt))
        psb = lambda name, shape: ctx.enter_context(nc.psum_tensor(name, shape, F32))

        xd = sb("xd", [128, 2 * TT * D])
        b_lnFh = sb("b_lnFh", [128, 8 * G], BF16)
        b_lnFl = sb("b_lnFl", [128, 8 * G], BF16)
        b_lnF = sb("b_lnF32", [128, 8 * G])
        b_aux = sb("b_aux", [128, 8 * G])
        b_kgF = sb("b_kgF", [128, 8 * G])
        b_kgT = sb("b_kgT", [128, TT * D])
        b_v = sb("b_v", [128, TT * D])
        b_og = b_lnF
        ln_tm = sb("ln_tm", [128, TT * D])
        b_f = ln_tm
        o_tm = sb("o_tm", [128, D])
        wh0 = sb("wh0", [128, 8 * 512], BF16)
        wl0 = sb("wl0", [128, 8 * 512], BF16)
        wh1 = sb("wh1", [128, 8 * 512], BF16)
        wl1 = sb("wl1", [128, 8 * 512], BF16)
        wslots = [(wh0, wl0), (wh1, wl1)]
        gw_sb = sb("gw_sb", [128, 8 * E])
        wd_sb = sb("wd_sb", [128, 8 * H])
        bd_sb = sb("bd_sb", [H, 1])
        nwb = sb("nwb", [128, D])
        i128 = sb("i128_sb", [128, 128])
        tri = sb("tri_sb", [128, 128])
        ones16 = sb("ones16_sb", [16, 128])
        cmask = sb("cmask_sb", [128, 1])
        umask = sb("umask_sb", [128, 1])
        csb = sb("consts_sb", [128, 4])
        s_a = sb("s_a", [128, (H // 2) * DH])
        s_b = sb("s_b", [128, (H // 2) * DH])
        s_bufs = [s_a, s_b]
        m_tm = sb("m_tm", [128, TT * H])
        t_alpha = sb("t_alpha", [128, TT * H])
        t_inva = sb("t_inva", [128, TT * H])
        t_beta = sb("t_beta", [128, TT * H])
        t_lam = sb("t_lam", [128, TT * H])
        St4 = sb("St4", [128, 512])
        kgb_all = sb("kgb_all", [128, D])
        kgb_bufs = (kgb_all, o_tm)
        zscr = sb("zscr", [128, 4 * E])
        s0s1 = sb("s0s1", [128, 2 * TT])
        stat = sb("stat", [128, 24])
        burn = None
        sp_scr = sb("sp_scr", [H, 2 * G])
        m_gr = sp_scr[:, G:2 * G]
        mc_rhs = sp_scr[:, 0:64]

        ps_mm = psb("ps_mm", [128, 2048])
        ps_tr = psb("ps_tr", [128, 512])
        ps_sc = psb("ps_sc", [128, 512])
        ps_o = psb("ps_o", [128, 512])
        ps_s = psb("ps_s", [128, 512])

        sems = {}
        for s in ("qw0", "qw1", "qx0", "qx1", "qp0", "qp1", "qs", "qs2", "qs3", "qg", "t", "v", "a", "tw", "cc"):
            sems[s] = ctx.enter_context(nc.semaphore(s))
        block = ctx.enter_context(nc.Block())
        p = P(nc, sems)

        a_ms = ["e0", "e1", "ow", "wg", "wk", "wv"]
        if cfg.stop_after == "ffn1":
            a_ms = ["e0", "e1", "ow"]
        b_ms = ["wq", "wog", "wo", "e0", "e1", "ow"]
        if cfg.stop_after in ("ffn1", "proj"):
            b_ms = []
        wload_seq = []
        for g in range(NG):
            for mname in a_ms:
                wload_seq.append((mname, 0))
                wload_seq.append((mname, 1))
        for g in range(NG):
            for mname in b_ms:
                wload_seq.append((mname, 0))
                wload_seq.append((mname, 1))

        ws = {"loaded": 0, "used": 0, "tfree": 0}

        def w_dram_half(dram, half):
            apx = dram[:, half * 512:(half + 1) * 512]
            return apx.rearrange("(k p) c -> p k c", p=128)

        def wslot_view(buf):
            return buf[:].rearrange("p (k c) -> p k c", c=512)

        def w_issue_one():
            k = ws["loaded"]
            if k >= len(wload_seq):
                return
            mname, half = wload_seq[k]
            sem = "qw0" if k % 2 == 0 else "qw1"
            wh, wl = wslots[k % 2]
            p.do("sync", lambda e, wh=wh, mname=mname, half=half: e.dma_start(
                out=wslot_view(wh), in_=w_dram_half(whi_d[mname], half)), inc=sem, n=16)
            p.wait("scalar", "t", ws["tfree"])
            p.do("scalar", lambda e, wl=wl, mname=mname, half=half: e.dma_start(
                out=wslot_view(wl), in_=w_dram_half(wlo_d[mname], half)), inc=sem, n=16)
            ws["loaded"] += 1

        def w_use_begin(k):
            base = ws["qw0_base"] if k % 2 == 0 else ws["qw1_base"]
            p.wait_future("tensor", "qw0" if k % 2 == 0 else "qw1",
                          base + 32 * (k // 2 + 1))
            return wslots[k % 2]

        def w_use_end(k):
            ws["used"] += 1
            ws["tfree"] = p.c["t"]
            p.wait("sync", "t", p.c["t"])
            w_issue_one()

        def dma(qeng, out, in_, sem):
            p.do(qeng, lambda e, out=out, in_=in_: e.dma_start(out=out, in_=in_), inc=sem, n=16)

        def xgrp(slot):
            return xd[:, slot * TT * D:(slot + 1) * TT * D]

        def x_tt(slot, tt):
            return xd[:, slot * TT * D + tt * D: slot * TT * D + (tt + 1) * D]

        def tokmaj_dram_view(dram, g):
            return dram[g * G:(g + 1) * G, :].rearrange("(t p) c -> p t c", p=128)

        def featmaj_dram_view(dram, g):
            return dram[:, g * G:(g + 1) * G].rearrange("(ch p) t -> p ch t", p=128)

        def featmaj_sb_view(buf):
            return buf[:].rearrange("p (ch t) -> p ch t", t=G)

        def tokmaj_sb_view(buf):
            return buf[:, 0:TT * D].rearrange("p (t c) -> p t c", c=D)

        def weight_prologue():
            names = ["e0", "e1", "ow", "wg", "wk", "wv", "wq", "wog", "wo"]
            steps = [(nm, r) for nm in names for r in range(8)]
            flat = (wslots[0][0], wslots[0][1], wslots[1][0], wslots[1][1])
            npar = min(4, (8 * G) // 1024)
            qws = ("qw0", "qw1", "qx0", "qx1")[:npar]
            qps = ("qp0", "qp1", "qs", "qs2")[:npar]
            qw_base = [p.c[s] for s in qws]
            qp_base = [p.c[s] for s in qps]
            vmarks = []
            for i, (nm, r) in enumerate(steps):
                par = i % npar
                qw = qws[par]
                qp = qps[par]
                f32buf = b_aux[:, par * 1024:(par + 1) * 1024]
                hibuf = flat[par][:, 0:1024]
                lobuf = flat[par][:, 1024:2048]
                rows = wmats[nm][r * 128:(r + 1) * 128, :]
                if i >= npar:
                    p.wait("sync", "v", vmarks[i - npar])
                p.do("sync", lambda e, f32buf=f32buf, rows=rows: e.dma_start(
                    out=f32buf.rearrange("p (k c) -> p k c", c=D),
                    in_=rows.rearrange("(k p) c -> p k c", p=128)), inc=qw, n=16)
                qv = qw_base[par] + 16 * (i // npar + 1)
                pv = qp_base[par] + 32 * (i // npar)
                p.wait("scalar", qw, qv)
                p.wait("scalar", qp, pv)
                p.do("scalar", lambda e, hibuf=hibuf, f32buf=f32buf: e.copy(
                    out=hibuf, in_=f32buf))
                am = p.mark("scalar", "a")
                p.wait("vector", "a", am)
                p.wait("vector", qp, pv)
                p.do("vector", lambda e, lobuf=lobuf, f32buf=f32buf, hibuf=hibuf: e.tensor_tensor(
                    out=lobuf, in0=f32buf, in1=hibuf, op=OP.subtract))
                mk = p.mark("vector", "v")
                vmarks.append(mk)
                p.wait("gpsimd", "v", mk)
                dsth = whi_d[nm][r * 128:(r + 1) * 128, :]
                dstl = wlo_d[nm][r * 128:(r + 1) * 128, :]
                dma("gpsimd", dsth.rearrange("(k p) c -> p k c", p=128),
                    hibuf.rearrange("p (k c) -> p k c", c=D), qp)
                dma("gpsimd", dstl.rearrange("(k p) c -> p k c", p=128),
                    lobuf.rearrange("p (k c) -> p k c", c=D), qp)
            for s in qps:
                p.wait("sync", s, p.c[s])
                p.wait("scalar", s, p.c[s])

        def ln_tile(slot, tt):
            p.wait("vector", "v", p.c["v"])
            p.wait("scalar", "v", p.c["v"])
            xa = x_tt(slot, tt)
            p.do("vector", lambda e, xa=xa: e.bn_stats(
                out=stat[:, 0:6], in_=xa[:, 0:512]))
            p.do("vector", lambda e, xa=xa: e.bn_stats(
                out=stat[:, 6:12], in_=xa[:, 512:1024]))
            p.drain("vector")
            c0 = 12 + 3 * tt
            p.do("vector", lambda e, c0=c0: e.bn_aggr(out=stat[:, c0:c0 + 2], in_=stat[:, 0:12]))
            mkv = p.mark("vector", "v")
            p.wait("scalar", "v", mkv)
            p.do("scalar", lambda e, c0=c0: e.activation(
                out=stat[:, c0 + 1:c0 + 2], in_=stat[:, c0 + 1:c0 + 2], func=AF.Sqrt,
                bias=csb[:, 0:1], scale=1.0))
            am = p.mark("scalar", "a")
            p.wait("vector", "a", am)
            p.do("vector", lambda e, c0=c0: e.reciprocal(
                out=stat[:, c0 + 1:c0 + 2], in_=stat[:, c0 + 1:c0 + 2]))
            p.drain("vector")
            p.do("vector", lambda e, c0=c0: e.tensor_scalar(
                out=stat[:, c0 + 2:c0 + 3], in0=stat[:, c0:c0 + 1],
                scalar1=stat[:, c0 + 1:c0 + 2],
                scalar2=-1.0, op0=OP.mult, op1=OP.mult))
            mkv = p.mark("vector", "v")
            p.wait("scalar", "v", mkv)
            p.do("scalar", lambda e, xa=xa, tt=tt, c0=c0: e.activation(
                out=ln_tm[:, tt * D:(tt + 1) * D], in_=xa, func=AF.Identity,
                scale=stat[:, c0 + 1:c0 + 2], bias=stat[:, c0 + 2:c0 + 3]))
            return ("a", p.mark("scalar", "a"))

        def ln_stats_apply(slot):
            p.wait("scalar", "v", p.c["v"])
            p.wait("vector", "v", p.c["v"])
            return [ln_tile(slot, tt) for tt in range(TT)]

        def transpose_to_featmaj(src_tm, dst_h, dst_l, dst_f32=None, tt_marks=None):
            vmarks = []
            tbanks = (ps_tr, ps_sc, ps_o, ps_s)
            nset = 0
            for tt in range(TT):
                if tt_marks is not None:
                    p.wait("tensor", tt_marks[tt][0], tt_marks[tt][1])
                for chg in range(2):
                    pb = tbanks[nset % 4]
                    if nset >= 4:
                        p.wait("tensor", "v", vmarks[nset - 4])
                    for cc in range(4):
                        ch = chg * 4 + cc
                        p.do("tensor", lambda e, ch=ch, tt=tt, cc=cc, pb=pb: e.matmul(
                            pb[:, cc * 128:(cc + 1) * 128],
                            src_tm[:, tt * D + ch * 128: tt * D + (ch + 1) * 128],
                            i128[:], start=(cc == 0), stop=(cc == 3)),
                            inc="t" if cc == 3 else None)
                    tm = p.c["t"]
                    p.wait("scalar", "t", tm)
                    pbv = pb[:, 0:512].rearrange("p (c f) -> p c f", f=128)
                    dh = dst_h[:].rearrange("p (c f) -> p c f", f=G)[:, chg * 4:chg * 4 + 4, tt * 128:(tt + 1) * 128]
                    p.do("scalar", lambda e, pbv=pbv, dh=dh: e.copy(out=dh, in_=pbv))
                    if dst_f32 is not None:
                        df = dst_f32[:].rearrange("p (c f) -> p c f", f=G)[:, chg * 4:chg * 4 + 4, tt * 128:(tt + 1) * 128]
                        p.do("scalar", lambda e, pbv=pbv, df=df: e.copy(out=df, in_=pbv))
                    am = p.mark("scalar", "a")
                    p.wait("vector", "a", am)
                    dl = dst_l[:].rearrange("p (c f) -> p c f", f=G)[:, chg * 4:chg * 4 + 4, tt * 128:(tt + 1) * 128]
                    dh2 = dst_h[:].rearrange("p (c f) -> p c f", f=G)[:, chg * 4:chg * 4 + 4, tt * 128:(tt + 1) * 128]
                    p.do("vector", lambda e, pbv=pbv, dl=dl, dh2=dh2: e.tensor_tensor(
                        out=dl, in0=pbv, in1=dh2, op=OP.subtract))
                    vmarks.append(p.mark("vector", "v"))
                    nset += 1
            p.wait("tensor", "v", p.c["v"])
            p.wait("tensor", "a", p.c["a"])

        def mm_tokmaj(Lh, Ll, k0, evac_fn):
            marks = []
            idx = 0
            for half in range(2):
                wh, wl = w_use_begin(k0 + half)
                for tt in range(TT):
                    bank = (idx % 4) * 512
                    if idx >= 4:
                        p.wait("tensor", marks[idx - 4][0], marks[idx - 4][1])
                    first = True
                    for A, W in ((Lh, wh), (Ll, wh), (Lh, wl)):
                        for ch in range(8):
                            last = (A is Lh and W is wl and ch == 7)
                            p.do("tensor", lambda e, ch=ch, tt=tt, A=A, W=W, bank=bank,
                                 first=first, last=last: e.matmul(
                                ps_mm[:, bank:bank + 512],
                                A[:, ch * G + tt * 128: ch * G + (tt + 1) * 128],
                                W[:, ch * 512:(ch + 1) * 512],
                                start=first, stop=last),
                                inc="t" if last else None)
                            first = False
                    marks.append(evac_fn(tt, half, ps_mm[:, bank:bank + 512]))
                    idx += 1
                w_use_end(k0 + half)
            p.wait("tensor", "v", p.c["v"])
            p.wait("tensor", "a", p.c["a"])
            return k0 + 2

        def mm_featmaj(Lh, Ll, k0, evac_fn):
            marks = []
            idx = 0
            for half in range(2):
                wh, wl = w_use_begin(k0 + half)
                for oc in range(4):
                    och = half * 4 + oc
                    bank = (idx % 4) * 512
                    if idx >= 4:
                        p.wait("tensor", marks[idx - 4][0], marks[idx - 4][1])
                    first = True
                    for W, A in ((wh, Lh), (wh, Ll), (wl, Lh)):
                        for ch in range(8):
                            last = (W is wl and ch == 7)
                            p.do("tensor", lambda e, ch=ch, oc=oc, W=W, A=A, bank=bank,
                                 first=first, last=last: e.matmul(
                                ps_mm[:, bank:bank + G],
                                W[:, ch * 512 + oc * 128: ch * 512 + (oc + 1) * 128],
                                A[:, ch * G:(ch + 1) * G],
                                start=first, stop=last),
                                inc="t" if last else None)
                            first = False
                    marks.append(evac_fn(och, ps_mm[:, bank:bank + G]))
                    idx += 1
                w_use_end(k0 + half)
            p.wait("tensor", "v", p.c["v"])
            p.wait("tensor", "a", p.c["a"])
            return k0 + 2

        def softmax_gate(lnF_src):
            zbanks = (ps_tr, ps_sc, ps_o, ps_s)
            zmarks = []
            for tt in range(TT):
                zb = zbanks[tt % 4]
                for ch in range(8):
                    p.do("tensor", lambda e, ch=ch, tt=tt, zb=zb: e.matmul(
                        zb[:, 0:E],
                        lnF_src[:, ch * G + tt * 128: ch * G + (tt + 1) * 128],
                        gw_sb[:, ch * E:(ch + 1) * E],
                        start=(ch == 0), stop=(ch == 7)),
                        inc="t" if ch == 7 else None)
                zmarks.append(p.c["t"])
            sm_marks = []
            for tt in range(TT):
                zb = zbanks[tt % 4]
                p.wait("vector", "t", zmarks[tt])
                p.do("vector", lambda e, zb=zb: e.reduce_max(
                    out=zscr[:, 0:1], in_=zb[:, 0:E], axis=AX.X))
                p.drain("vector")
                p.do("vector", lambda e: e.tensor_scalar(
                    out=zscr[:, 0:1], in0=zscr[:, 0:1], scalar1=-1.0, scalar2=None, op0=OP.mult))
                mk = p.mark("vector", "v")
                p.wait("scalar", "v", mk)
                p.do("scalar", lambda e, zb=zb: e.activation(
                    out=zscr[:, 8:16], in_=zb[:, 0:E], func=AF.Exp,
                    bias=zscr[:, 0:1], scale=1.0))
                am = p.mark("scalar", "a")
                p.wait("vector", "a", am)
                p.do("vector", lambda e: e.reduce_sum(
                    out=zscr[:, 1:2], in_=zscr[:, 8:16], axis=AX.X))
                p.drain("vector")
                p.do("vector", lambda e: e.reciprocal(out=zscr[:, 1:2], in_=zscr[:, 1:2]))
                p.drain("vector")
                p.do("vector", lambda e: e.tensor_scalar(
                    out=zscr[:, 16:24], in0=zscr[:, 8:16], scalar1=zscr[:, 1:2],
                    scalar2=None, op0=OP.mult))
                p.drain("vector")
                p.do("vector", lambda e: e.tensor_scalar(
                    out=zscr[:, 24:32], in0=zscr[:, 16:24], scalar1=TOP_P,
                    scalar2=None, op0=OP.is_gt))
                p.drain("vector")
                p.do("vector", lambda e: e.reduce_sum(
                    out=zscr[:, 2:3], in_=zscr[:, 24:32], axis=AX.X))
                p.drain("vector")
                p.do("vector", lambda e: e.tensor_scalar(
                    out=zscr[:, 3:4], in0=zscr[:, 2:3], scalar1=-1.0, scalar2=float(E),
                    op0=OP.mult, op1=OP.add))
                p.drain("vector")
                p.do("vector", lambda e, tt=tt: e.tensor_tensor(
                    out=s0s1[:, tt:tt + 1], in0=zscr[:, 3:4], in1=zscr[:, 16:17], op=OP.mult))
                p.do("vector", lambda e, tt=tt: e.tensor_tensor(
                    out=s0s1[:, TT + tt:TT + tt + 1], in0=zscr[:, 2:3], in1=zscr[:, 17:18],
                    op=OP.mult))
                sm_marks.append(p.mark("vector", "v"))
            p.wait("vector", "v", sm_marks[-1])
            return p.c["a"]

        def tables_from_m():
            for tt in range(TT):
                p.do("tensor", lambda e, tt=tt: e.matmul(
                    ps_tr[:, tt * H:(tt + 1) * H],
                    m_gr[:, tt * 128:(tt + 1) * 128],
                    i128[0:H, 0:H], start=(tt == 0), stop=(tt == TT - 1)),
                    inc="t" if tt == TT - 1 else None)
            p.wait("vector", "t", p.c["t"])
            p.do("vector", lambda e: e.tensor_copy(out=m_tm[:, 0:TT * H], in_=ps_tr[:, 0:TT * H]))
            mk = p.mark("vector", "v")
            p.wait("tensor", "v", mk)
            p.wait("scalar", "v", mk)
            p.do("scalar", lambda e: e.activation(
                out=t_alpha[:, 0:TT * H], in_=m_tm[:, 0:TT * H], func=AF.Exp, scale=-1.0))
            p.do("scalar", lambda e: e.activation(
                out=t_inva[:, 0:TT * H], in_=m_tm[:, 0:TT * H], func=AF.Exp, scale=1.0))
            p.mark("scalar", "a")
            for tt in range(TT):
                p.do("vector", lambda e, tt=tt: e.tensor_scalar(
                    out=mc_rhs[:, tt * H:(tt + 1) * H], in0=i128[0:H, 0:H],
                    scalar1=m_gr[:, tt * 128 + 127: tt * 128 + 128],
                    scalar2=None, op0=OP.mult))
            mk = p.mark("vector", "v")
            p.wait("tensor", "v", mk)
            p.do("tensor", lambda e: e.matmul(
                ps_tr[:, 0:TT * H], ones16[:], mc_rhs[:, 0:TT * H],
                start=True, stop=True), inc="t")
            tm = p.c["t"]
            p.wait("vector", "t", tm)
            p.wait("scalar", "t", tm)
            p.do("vector", lambda e: e.tensor_tensor(
                out=t_beta[:, 0:TT * H], in0=m_tm[:, 0:TT * H], in1=ps_tr[:, 0:TT * H],
                op=OP.subtract))
            mk = p.mark("vector", "v")
            p.wait("scalar", "v", mk)
            p.do("scalar", lambda e: e.activation(
                out=t_beta[:, 0:TT * H], in_=t_beta[:, 0:TT * H], func=AF.Exp, scale=1.0))
            p.do("scalar", lambda e: e.activation(
                out=t_lam[:, 0:TT * H], in_=ps_tr[:, 0:TT * H], func=AF.Exp, scale=-1.0))
            am = p.mark("scalar", "a")
            p.wait("tensor", "a", am)
            p.wait("vector", "a", am)

        def decay_stage(g, lnF_src, spill=True):
            for ch in range(8):
                p.do("tensor", lambda e, ch=ch: e.matmul(
                    ps_tr[0:H, 0:G],
                    wd_sb[:, ch * H:(ch + 1) * H],
                    lnF_src[:, ch * G:(ch + 1) * G],
                    start=(ch == 0), stop=(ch == 7)),
                    inc="t" if ch == 7 else None)
            tm = p.c["t"]
            p.wait("scalar", "t", tm)
            r_ap = sp_scr[:, 0:G]
            a_ap = sp_scr[:, G:2 * G]
            l_ap = sp_scr[:, G:2 * G]
            m_raw = sp_scr[:, 0:G]
            p.do("scalar", lambda e: e.activation(
                out=r_ap, in_=ps_tr[0:H, 0:G], func=AF.Relu, bias=bd_sb[:], scale=1.0))
            p.do("scalar", lambda e: e.activation(
                out=a_ap, in_=ps_tr[0:H, 0:G], func=AF.Abs, bias=bd_sb[:], scale=1.0))
            p.drain("scalar")
            p.do("scalar", lambda e: e.activation(
                out=a_ap, in_=a_ap, func=AF.Exp, scale=-1.0))
            p.drain("scalar")
            p.do("scalar", lambda e: e.activation(
                out=l_ap, in_=a_ap, func=AF.Ln, bias=csb[0:H, 2:3], scale=1.0))
            am = p.mark("scalar", "a")
            p.wait("vector", "a", am)
            p.wait("tensor", "a", am)
            p.do("vector", lambda e: e.tensor_tensor(
                out=m_raw, in0=r_ap, in1=l_ap, op=OP.add))
            p.drain("vector")
            for tt in range(TT):
                p.do("vector", lambda e, tt=tt: e.tensor_tensor_scan(
                    out=m_gr[:, tt * 128:(tt + 1) * 128],
                    data0=sp_scr[:, tt * 128:(tt + 1) * 128],
                    data1=sp_scr[:, tt * 128:(tt + 1) * 128],
                    initial=0.0, op0=OP.add, op1=OP.bypass))
            mk = p.mark("vector", "v")
            if spill:
                p.wait("gpsimd", "v", mk)
                dma("gpsimd", sp_m[:, g * G:(g + 1) * G], m_gr[:], "qg")
            p.wait("tensor", "v", mk)

        def gla_kgb_batch(tt, buf):
            p.wait("scalar", "t", p.c["t"])
            p.wait("scalar", "a", p.c["a"])
            p.wait("scalar", "qg", p.c["qg"])
            for h in range(H):
                col = tt * H + h
                p.do("scalar", lambda e, tt=tt, h=h, col=col, buf=buf: e.activation(
                    out=buf[:, h * DH:(h + 1) * DH],
                    in_=b_kgT[:, tt * D + h * DH: tt * D + (h + 1) * DH],
                    func=AF.Copy, scale=t_beta[:, col:col + 1]))
            amk = p.mark("scalar", "a")
            return (amk, p.c["v"])

        def gla_state_chunk(gchunk, tt, g=None, pre=None):
            cur = s_bufs[gchunk % 2]
            nxt = s_bufs[(gchunk + 1) % 2]
            amk, buf = pre
            if g is not None:
                p.wait("gpsimd", "a", amk)
                rows = sp_kgT[g * G + tt * 128: g * G + (tt + 1) * 128, :]
                dma("gpsimd", rows, buf[:], "qg")
            nxt_pre = None
            if tt + 1 < TT:
                a2, _ = gla_kgb_batch(tt + 1, kgb_bufs[(tt + 1) % 2])
                nxt_pre = (a2, kgb_bufs[(tt + 1) % 2])
            kgb_all_local = buf
            p.wait("tensor", "a", amk)
            sd_banks = (ps_sc, ps_s, ps_tr, ps_o)
            ev = []
            for hp in range(H // 2):
                h = 2 * hp
                sdb = sd_banks[hp % 4]
                if hp >= 4:
                    p.wait("tensor", "v", ev[hp - 4])
                p.do("tensor", lambda e, tt=tt, h=h, sdb=sdb, buf=kgb_all_local: e.matmul(
                    sdb[:, 0:128], buf[:, h * DH:(h + 2) * DH],
                    b_v[:, tt * D + h * DH: tt * D + (h + 2) * DH],
                    start=True, stop=True), inc="t")
                tm = p.c["t"]
                p.wait("vector", "t", tm)
                for i in (0, 1):
                    hh = h + i
                    hb = (hh % 2) * 64
                    hc = (hh // 2) * 64
                    col = tt * H + hh
                    p.do("vector", lambda e, cur=cur, nxt=nxt, hb=hb, hc=hc, col=col,
                         sdb=sdb, i=i: e.scalar_tensor_tensor(
                             out=nxt[hb:hb + 64, hc:hc + 64],
                             in0=cur[hb:hb + 64, hc:hc + 64],
                             scalar=t_lam[hb:hb + 64, col:col + 1],
                             in1=sdb[i * 64:(i + 1) * 64, i * 64:(i + 1) * 64],
                             op0=OP.mult, op1=OP.add))
                ev.append(p.mark("vector", "v"))
            p.wait("tensor", "v", p.c["v"])
            return nxt_pre

        def gla_out_chunk(gchunk, tt, qF_src, kgF_src):
            cur = s_bufs[gchunk % 2]
            nxt = s_bufs[(gchunk + 1) % 2]

            def qk(h):
                hb = (h % 2) * 64
                c0 = (h // 2) * G + tt * 128
                return (qF_src[hb:hb + 64, c0:c0 + 128],
                        kgF_src[hb:hb + 64, c0:c0 + 128])

            m_st_pre = [None] * H

            sc_marks = [None] * H

            def emit_score(h):
                b = h % 4
                qsl, ksl = qk(h)
                if h >= 4:
                    p.wait("tensor", "v", m_st_pre[h - 4])
                p.do("tensor", lambda e, b=b, ksl=ksl, qsl=qsl: e.matmul(
                    ps_mm[:, b * 512: b * 512 + 128],
                    ksl, qsl, start=True, stop=True), inc="t")
                sc_marks[h] = p.c["t"]

            for h in range(4):
                emit_score(h)
            m_intra = [None] * H
            m_oe = [None] * H
            ob_banks = (ps_sc, ps_s)
            sd_banks = (ps_tr, ps_o)
            sdp_marks = []
            for h in range(H):
                b = h % 4
                qsl, ksl = qk(h)
                hb = (h % 2) * 64
                hc = (h // 2) * 64
                col = tt * H + h
                stb = St4[:, b * 128:(b + 1) * 128]
                obb = ob_banks[h % 2]
                p.wait("vector", "t", sc_marks[h])
                if h >= 4:
                    p.wait("vector", "t", m_intra[h - 4])
                p.do("vector", lambda e, col=col, b=b, stb=stb: e.scalar_tensor_tensor(
                    out=stb, in0=ps_mm[:, b * 512: b * 512 + 128],
                    scalar=t_inva[:, col:col + 1],
                    in1=tri[:], op0=OP.mult, op1=OP.mult))
                m_st_pre[h] = p.mark("vector", "v")
                if h + 4 < H:
                    emit_score(h + 4)
                if h == 0 and wo_state["mark"] is not None:
                    p.wait("tensor", "v", wo_state["mark"])
                if h >= 2:
                    p.wait("tensor", "a", m_oe[h - 2])
                p.do("tensor", lambda e, qsl=qsl, cur=cur, hb=hb, hc=hc, obb=obb: e.matmul(
                    obb[:, 0:DH], qsl, cur[hb:hb + 64, hc:hc + 64],
                    start=True, stop=False))
                p.wait("tensor", "v", m_st_pre[h])
                p.do("tensor", lambda e, tt=tt, h=h, stb=stb, obb=obb: e.matmul(
                    obb[:, 0:DH], stb,
                    b_v[:, tt * D + h * DH: tt * D + (h + 1) * DH],
                    start=False, stop=True), inc="t")
                m_intra[h] = p.c["t"]
                p.wait("scalar", "t", m_intra[h])
                p.do("scalar", lambda e, h=h, col=col, obb=obb: e.activation(
                    out=o_tm[:, h * DH:(h + 1) * DH], in_=obb[:, 0:DH],
                    func=AF.Copy, scale=t_alpha[:, col:col + 1]))
                m_oe[h] = p.mark("scalar", "a")
                if h % 2 == 1:
                    hp = h // 2
                    sdb = sd_banks[hp % 2]
                    if hp >= 2:
                        p.wait("tensor", "v", sdp_marks[hp - 2])
                    p.do("tensor", lambda e, tt=tt, h=h, sdb=sdb: e.matmul(
                        sdb[:, 0:128], b_kgT[:, tt * D + (h - 1) * DH: tt * D + (h + 1) * DH],
                        b_v[:, tt * D + (h - 1) * DH: tt * D + (h + 1) * DH],
                        start=True, stop=True), inc="t")
                    tm = p.c["t"]
                    p.wait("vector", "t", tm)
                    for i in (0, 1):
                        hh = h - 1 + i
                        hhb = (hh % 2) * 64
                        hhc = (hh // 2) * 64
                        ccol = tt * H + hh
                        p.do("vector", lambda e, cur=cur, nxt=nxt, hhb=hhb, hhc=hhc,
                             ccol=ccol, sdb=sdb, i=i: e.scalar_tensor_tensor(
                                 out=nxt[hhb:hhb + 64, hhc:hhc + 64],
                                 in0=cur[hhb:hhb + 64, hhc:hhc + 64],
                                 scalar=t_lam[hhb:hhb + 64, ccol:ccol + 1],
                                 in1=sdb[i * 64:(i + 1) * 64, i * 64:(i + 1) * 64],
                                 op0=OP.mult, op1=OP.add))
                    sdp_marks.append(p.mark("vector", "v"))
            p.wait("tensor", "v", p.c["v"])
            p.wait("scalar", "v", p.c["v"])

        def phase_a_group(g):
            slot = g % 2
            qg_prev = p.c["qg"]
            p.wait("vector", "qg", qg_prev)
            p.wait("scalar", "qg", qg_prev)
            if g + 1 < NG:
                p.wait("gpsimd", "v", p.c["v"])
                p.wait("gpsimd", "a", p.c["a"])
                sem = "qx0" if (g + 1) % 2 == 0 else "qx1"
                dma("gpsimd", xgrp((g + 1) % 2).rearrange("p (t c) -> p t c", c=D),
                    tokmaj_dram_view(x_in, g + 1), sem)
                xwm[g + 1] = (sem, p.c[sem])
            if g > 0:
                p.wait("vector", xwm[g][0], xwm[g][1])
                p.wait("scalar", xwm[g][0], xwm[g][1])
            lnm = ln_stats_apply(slot)
            transpose_to_featmaj(ln_tm, b_lnFh, b_lnFl, b_lnF, tt_marks=lnm)
            softmax_gate(b_lnF)
            k0 = ws["used"]

            def evac_e0(tt, half, ps):
                p.wait("scalar", "t", p.c["t"])
                p.wait("scalar", "v", p.c["v"])
                p.do("scalar", lambda e, tt=tt, half=half, ps=ps: e.activation(
                    out=b_f[:, tt * D + half * 512: tt * D + half * 512 + 512],
                    in_=ps, func=AF.Identity, scale=s0s1[:, tt:tt + 1]))
                return ("a", p.mark("scalar", "a"))
            k0 = mm_tokmaj(b_lnFh, b_lnFl, k0, evac_e0)

            def evac_e1(tt, half, ps):
                p.wait("vector", "t", p.c["t"])
                p.do("vector", lambda e, tt=tt, half=half, ps=ps: e.scalar_tensor_tensor(
                    out=b_f[:, tt * D + half * 512: tt * D + half * 512 + 512],
                    in0=ps, scalar=s0s1[:, TT + tt:TT + tt + 1],
                    in1=b_f[:, tt * D + half * 512: tt * D + half * 512 + 512],
                    op0=OP.mult, op1=OP.add))
                return ("v", p.mark("vector", "v"))
            k0 = mm_tokmaj(b_lnFh, b_lnFl, k0, evac_e1)

            transpose_to_featmaj(b_f, b_lnFh, b_lnFl)

            def evac_out(tt, half, ps):
                p.wait("vector", "t", p.c["t"])
                dst = xd[:, slot * TT * D + tt * D + half * 512:
                         slot * TT * D + tt * D + half * 512 + 512]
                p.do("vector", lambda e, ps=ps, dst=dst: e.scalar_tensor_tensor(
                    out=dst, in0=ps, scalar=0.5, in1=dst, op0=OP.mult, op1=OP.add))
                return ("v", p.mark("vector", "v"))
            k0 = mm_tokmaj(b_lnFh, b_lnFl, k0, evac_out)

            p.wait("gpsimd", "v", p.c["v"])
            dma("gpsimd", tokmaj_dram_view(sp_x1, g),
                xgrp(slot).rearrange("p (t c) -> p t c", c=D), "qg")
            if cfg.stop_after == "ffn1":
                dma("gpsimd", tokmaj_dram_view(y_out, g), xgrp(slot).rearrange(
                    "p (t c) -> p t c", c=D), "qg")
                return

            lnm = ln_stats_apply(slot)
            transpose_to_featmaj(ln_tm, b_lnFh, b_lnFl, b_lnF, tt_marks=lnm)
            p.wait("gpsimd", "v", p.c["v"])
            p.wait("gpsimd", "a", p.c["a"])
            dma("gpsimd", featmaj_dram_view(sp_lnFh, g), featmaj_sb_view(b_lnFh), "qg")
            dma("gpsimd", featmaj_dram_view(sp_lnFl, g), featmaj_sb_view(b_lnFl), "qg")

            def evac_g(och, ps):
                p.wait("scalar", "t", p.c["t"])
                p.do("scalar", lambda e, och=och, ps=ps: e.activation(
                    out=b_aux[:, och * G:(och + 1) * G], in_=ps, func=AF.Sigmoid))
                return ("a", p.mark("scalar", "a"))
            k0 = mm_featmaj(b_lnFh, b_lnFl, k0, evac_g)

            def evac_k(och, ps):
                p.wait("vector", "t", p.c["t"])
                p.wait("vector", "a", p.c["a"])
                p.do("vector", lambda e, och=och, ps=ps: e.tensor_tensor(
                    out=b_kgF[:, och * G:(och + 1) * G], in0=ps,
                    in1=b_aux[:, och * G:(och + 1) * G], op=OP.mult))
                return ("v", p.mark("vector", "v"))
            k0 = mm_featmaj(b_lnFh, b_lnFl, k0, evac_k)

            def evac_v(tt, half, ps):
                p.wait("scalar", "t", p.c["t"])
                p.do("scalar", lambda e, tt=tt, half=half, ps=ps: e.copy(
                    out=b_v[:, tt * D + half * 512: tt * D + half * 512 + 512], in_=ps))
                return ("a", p.mark("scalar", "a"))
            k0 = mm_tokmaj(b_lnFh, b_lnFl, k0, evac_v)

            p.wait("gpsimd", "v", p.c["v"])
            p.wait("gpsimd", "a", p.c["a"])
            dma("gpsimd", featmaj_dram_view(sp_kgF, g), featmaj_sb_view(b_kgF), "qg")
            dma("gpsimd", tokmaj_dram_view(sp_v, g), tokmaj_sb_view(b_v), "qg")

            decay_stage(g, b_lnF, spill=True)
            vmarks2 = []
            tbanks4 = (ps_tr, ps_sc, ps_o, ps_s)
            seti = 0
            for tt in range(TT):
                for ch in range(8):
                    pb = tbanks4[seti % 4]
                    p.do("tensor", lambda e, tt=tt, ch=ch, pb=pb: e.matmul(
                        pb[:, (ch % 4) * 128:(ch % 4) * 128 + 128],
                        b_kgF[:, ch * G + tt * 128: ch * G + (tt + 1) * 128],
                        i128[:], start=(ch % 4 == 0), stop=(ch % 4 == 3)),
                        inc="t" if ch % 4 == 3 else None)
                    if ch % 4 == 3:
                        p.wait("scalar", "t", p.c["t"])
                        base = (ch // 4) * 512
                        p.do("scalar", lambda e, tt=tt, base=base, pb=pb: e.copy(
                            out=b_kgT[:, tt * D + base: tt * D + base + 512],
                            in_=pb[:, 0:512]))
                        vmarks2.append(("a", p.mark("scalar", "a")))
                        seti += 1
                        if seti >= 4:
                            p.wait("tensor", vmarks2[seti - 4][0], vmarks2[seti - 4][1])
            p.wait("tensor", "v", p.c["v"])
            tables_from_m()
            if cfg.stop_after == "proj":
                return
            pre = gla_kgb_batch(0, kgb_bufs[0])
            pre = (pre[0], kgb_bufs[0])
            for tt in range(TT):
                pre = gla_state_chunk(g * TT + tt, tt, g=g, pre=pre)

        bwm = {}
        relay_wm = {}
        xwm = {}
        wo_state = {"mark": None}

        def issue_b_m(g):
            p.wait("gpsimd", "t", p.c["t"])
            p.wait("gpsimd", "v", p.c["v"])
            p.wait("gpsimd", "qg", p.c["qg"])
            dma("gpsimd", m_gr[:], sp_m[:, g * G:(g + 1) * G], "qs")
            bwm[(g, "m")] = p.c["qs"]

        def issue_b_big(g):
            slot = g % 2
            p.wait("gpsimd", "t", p.c["t"])
            p.wait("gpsimd", "v", p.c["v"])
            p.wait("gpsimd", "a", p.c["a"])
            p.wait("gpsimd", "qg", p.c["qg"])
            dma("gpsimd", xgrp(slot).rearrange("p (t c) -> p t c", c=D),
                tokmaj_dram_view(sp_x1, g), "qs2")
            dma("gpsimd", featmaj_sb_view(b_kgF), featmaj_dram_view(sp_kgF, g), "qs2")
            dma("gpsimd", tokmaj_sb_view(b_kgT), tokmaj_dram_view(sp_kgT, g), "qs2")
            dma("gpsimd", tokmaj_sb_view(b_v), tokmaj_dram_view(sp_v, g), "qs2")
            bwm[(g, "late")] = p.c["qs2"]

        def lnfpair(g):
            return (b_lnFh, b_lnFl)

        def issue_b_lnf(g):
            Lh, Ll = lnfpair(g)
            p.wait("sync", "qs3", p.c["qs3"])
            p.wait("sync", "t", p.c["t"])
            dma("sync", featmaj_sb_view(Lh), featmaj_dram_view(sp_lnFh, g), "qs3")
            p.wait("scalar", "qs3", p.c["qs3"] - 16)
            p.wait("scalar", "t", p.c["t"])
            dma("scalar", featmaj_sb_view(Ll), featmaj_dram_view(sp_lnFl, g), "qs3")
            bwm[(g, "lnf")] = p.c["qs3"]

        def issue_b_loads(g):
            issue_b_m(g)
            issue_b_big(g)
            issue_b_lnf(g)

        def phase_b_group(g):
            slot = g % 2
            Lh, Ll = lnfpair(g)
            qload = bwm[(g, "late")]
            for e in ("tensor", "vector", "scalar"):
                p.wait(e, "qs", bwm[(g, "m")])
            tables_from_m()
            if g + 1 < NG:
                issue_b_m(g + 1)
            for e in ("tensor", "vector", "scalar"):
                p.wait(e, "qs3", bwm[(g, "lnf")])
            k0 = ws["used"]

            def evac_q(och, ps):
                p.wait("scalar", "t", p.c["t"])
                p.do("scalar", lambda e, och=och, ps=ps: e.copy(
                    out=b_aux[:, och * G:(och + 1) * G], in_=ps))
                return ("a", p.mark("scalar", "a"))
            k0 = mm_featmaj(Lh, Ll, k0, evac_q)

            def evac_og(tt, half, ps):
                p.wait("scalar", "t", p.c["t"])
                p.do("scalar", lambda e, tt=tt, half=half, ps=ps: e.activation(
                    out=b_og[:, tt * D + half * 512: tt * D + half * 512 + 512],
                    in_=ps, func=AF.Sigmoid))
                return ("a", p.mark("scalar", "a"))
            k0 = mm_tokmaj(Lh, Ll, k0, evac_og)

            if g == 0:
                p.wait("vector", "qg", relay_wm["qg"])
                p.do("vector", lambda e: e.tensor_scalar(
                    out=s_bufs[0][:], in0=relay_wm["oth"][:], scalar1=umask[:],
                    scalar2=None, op0=OP.mult))
                p.mark("vector", "v")
            for e in ("tensor", "vector", "scalar"):
                p.wait(e, "qs2", qload)
            wo_h = (w_use_begin(k0), w_use_begin(k0 + 1))
            lnm = []
            wo_state["mark"] = None

            def emit_wo_tile(tt):
                for half in range(2):
                    wh, wl = wo_h[half]
                    wob = ps_sc if half == 0 else ps_s
                    first = True
                    for A, W in ((Lh, wh), (Ll, wh), (Lh, wl)):
                        for ch in range(8):
                            last = (A is Lh and W is wl and ch == 7)
                            p.do("tensor", lambda e, ch=ch, tt=tt, A=A, W=W, wob=wob,
                                 first=first, last=last: e.matmul(
                                wob[:, 0:512],
                                A[:, ch * G + tt * 128: ch * G + (tt + 1) * 128],
                                W[:, ch * 512:(ch + 1) * 512],
                                start=first, stop=last),
                                inc="t" if last else None)
                            first = False
                    p.wait("vector", "t", p.c["t"])
                    dst = xd[:, slot * TT * D + tt * D + half * 512:
                             slot * TT * D + tt * D + half * 512 + 512]
                    p.do("vector", lambda e, dst=dst, wob=wob: e.tensor_tensor(
                        out=dst, in0=wob[:, 0:512], in1=dst, op=OP.add))
                    wo_state["mark"] = p.mark("vector", "v")

            for tt in range(TT):
                gla_out_chunk(g * TT + tt, tt, b_aux, b_kgF)
                p.wait("scalar", "v", p.c["v"])
                p.wait("scalar", "a", p.c["a"])
                p.do("scalar", lambda e: e.activation(
                    out=kgb_all[:], in_=o_tm[:], func=AF.Square, accum_out=stat[:, 0:1]))
                p.drain("scalar")
                p.do("scalar", lambda e: e.activation(
                    out=stat[:, 0:1], in_=stat[:, 0:1], func=AF.Sqrt, bias=csb[:, 1:2],
                    scale=1.0 / D))
                am = p.mark("scalar", "a")
                p.wait("vector", "a", am)
                p.do("vector", lambda e: e.reciprocal(out=stat[:, 0:1], in_=stat[:, 0:1]))
                p.drain("vector")
                p.do("vector", lambda e: e.scalar_tensor_tensor(
                    out=o_tm[:], in0=o_tm[:], scalar=stat[:, 0:1], in1=nwb[:],
                    op0=OP.mult, op1=OP.mult))
                p.drain("vector")
                p.do("vector", lambda e, tt=tt: e.tensor_tensor(
                    out=o_tm[:], in0=o_tm[:], in1=b_og[:, tt * D:(tt + 1) * D], op=OP.mult))
                mk = p.mark("vector", "v")
                p.wait("tensor", "v", mk)
                for ch in range(8):
                    p.do("tensor", lambda e, ch=ch: e.matmul(
                        ps_tr[:, (ch % 4) * 128:(ch % 4) * 128 + 128],
                        o_tm[:, ch * 128:(ch + 1) * 128],
                        i128[:], start=(ch % 4 == 0), stop=(ch % 4 == 3)),
                        inc="t" if ch % 4 == 3 else None)
                    if ch % 4 == 3:
                        tmn = p.c["t"]
                        p.wait("scalar", "t", tmn)
                        for cc2 in range(4):
                            ch2 = (ch // 4) * 4 + cc2
                            p.do("scalar", lambda e, ch2=ch2, cc2=cc2, tt=tt: e.copy(
                                out=Lh[:, ch2 * G + tt * 128: ch2 * G + (tt + 1) * 128],
                                in_=ps_tr[:, cc2 * 128:(cc2 + 1) * 128]))
                        am = p.mark("scalar", "a")
                        p.wait("vector", "a", am)
                        for cc2 in range(4):
                            ch2 = (ch // 4) * 4 + cc2
                            p.do("vector", lambda e, ch2=ch2, cc2=cc2, tt=tt: e.tensor_tensor(
                                out=Ll[:, ch2 * G + tt * 128: ch2 * G + (tt + 1) * 128],
                                in0=ps_tr[:, cc2 * 128:(cc2 + 1) * 128],
                                in1=Lh[:, ch2 * G + tt * 128: ch2 * G + (tt + 1) * 128],
                                op=OP.subtract))
                        mk = p.mark("vector", "v")
                        p.wait("tensor", "v", mk)
                emit_wo_tile(tt)
                lnm.append(ln_tile(slot, tt))
            w_use_end(k0)
            w_use_end(k0 + 1)
            k0 = k0 + 2
            if g + 1 < NG:
                issue_b_big(g + 1)

            if wo_state["mark"] is not None:
                p.wait("tensor", "v", wo_state["mark"])
            transpose_to_featmaj(ln_tm, Lh, Ll, b_lnF, tt_marks=lnm)
            softmax_gate(b_lnF)

            def evac2_e0(tt, half, ps):
                p.wait("scalar", "t", p.c["t"])
                p.wait("scalar", "v", p.c["v"])
                p.do("scalar", lambda e, tt=tt, half=half, ps=ps: e.activation(
                    out=b_f[:, tt * D + half * 512: tt * D + half * 512 + 512],
                    in_=ps, func=AF.Identity, scale=s0s1[:, tt:tt + 1]))
                return ("a", p.mark("scalar", "a"))
            k0 = mm_tokmaj(Lh, Ll, k0, evac2_e0)

            def evac2_e1(tt, half, ps):
                p.wait("vector", "t", p.c["t"])
                p.do("vector", lambda e, tt=tt, half=half, ps=ps: e.scalar_tensor_tensor(
                    out=b_f[:, tt * D + half * 512: tt * D + half * 512 + 512],
                    in0=ps, scalar=s0s1[:, TT + tt:TT + tt + 1],
                    in1=b_f[:, tt * D + half * 512: tt * D + half * 512 + 512],
                    op0=OP.mult, op1=OP.add))
                return ("v", p.mark("vector", "v"))
            k0 = mm_tokmaj(Lh, Ll, k0, evac2_e1)

            transpose_to_featmaj(b_f, Lh, Ll)

            def evac2_out(tt, half, ps):
                p.wait("vector", "t", p.c["t"])
                dst = xd[:, slot * TT * D + tt * D + half * 512:
                         slot * TT * D + tt * D + half * 512 + 512]
                p.do("vector", lambda e, ps=ps, dst=dst: e.scalar_tensor_tensor(
                    out=dst, in0=ps, scalar=0.5, in1=dst, op0=OP.mult, op1=OP.add))
                return ("v", p.mark("vector", "v"))
            k0 = mm_tokmaj(Lh, Ll, k0, evac2_out)

            p.wait("gpsimd", "v", p.c["v"])
            dma("gpsimd", tokmaj_dram_view(y_out, g), xgrp(slot).rearrange(
                "p (t c) -> p t c", c=D), "qg")
            if g + 1 < NG:
                issue_b_lnf(g + 1)

        def program(pp):
            ws["loaded"] = 0
            ws["used"] = 0
            ws["tfree"] = 0
            dma("gpsimd", i128[:], i128_in[:], "qg")
            dma("gpsimd", tri[:], tri_in[:], "qg")
            dma("gpsimd", nwb[:], nwb_in[:], "qg")
            dma("gpsimd", ones16[:], ones16_in[:], "qg")
            dma("gpsimd", cmask[:], cmask_in[:], "qg")
            dma("gpsimd", umask[:], umask_in[:], "qg")
            dma("gpsimd", csb[:], consts_in[:], "qg")
            dma("gpsimd", bd_sb[:], bd_in[:], "qg")
            dma("gpsimd", gw_sb[:].rearrange("p (k c) -> p k c", c=E),
                gw[:].rearrange("(k p) c -> p k c", p=128), "qg")
            dma("gpsimd", wd_sb[:].rearrange("p (k c) -> p k c", c=H),
                wd[:].rearrange("(k p) c -> p k c", p=128), "qg")
            dma("gpsimd", xgrp(0).rearrange("p (t c) -> p t c", c=D),
                tokmaj_dram_view(x_in, 0), "qx0")
            xwm[0] = ("qx0", p.c["qx0"])
            cb = p.c["qg"]
            for e in ("vector", "scalar", "tensor", "gpsimd"):
                p.wait(e, "qg", cb)
            p.wait("vector", xwm[0][0], xwm[0][1])
            p.wait("scalar", xwm[0][0], xwm[0][1])
            p.do("vector", lambda e: e.memset(s_bufs[0][:], 0.0))
            p.mark("vector", "v")
            weight_prologue()
            ws["qw0_base"] = p.c["qw0"]
            ws["qw1_base"] = p.c["qw1"]
            w_issue_one()
            w_issue_one()
            for g in range(NG):
                phase_a_group(g)
            if cfg.stop_after in ("ffn1", "proj"):
                p.wait("gpsimd", "qg", p.c["qg"])
                return
            fin = s_bufs[NCH % 2]
            oth = s_bufs[(NCH + 1) % 2]
            p.wait("vector", "v", p.c["v"])
            p.do("vector", lambda e, fin=fin, oth=oth: e.tensor_scalar(
                out=oth[:], in0=fin[:], scalar1=cmask[:], scalar2=None, op0=OP.mult))
            mk = p.mark("vector", "v")
            p.wait("gpsimd", "v", mk)
            dma("gpsimd", cc_in[:], oth[:], "qg")
            issue_b_loads(0)
            p.wait("gpsimd", "qg", p.c["qg"])
            p.do("gpsimd", lambda e: e.collective_compute(
                "AllReduce", OP.add, replica_groups=cfg.pairs,
                ins=[cc_in.opt()], outs=[cc_out.opt()]), inc="cc", n=1)
            p.wait("gpsimd", "cc", p.c["cc"])
            dma("gpsimd", oth[:], cc_out[:], "qg")
            relay_wm["qg"] = p.c["qg"]
            relay_wm["oth"] = oth
            for g in range(NG):
                phase_b_group(g)
            p.wait("gpsimd", "qg", p.c["qg"])
            p.wait("gpsimd", "qs", p.c["qs"])
            p.wait("gpsimd", "qs2", p.c["qs2"])
            p.wait("gpsimd", "qs3", p.c["qs3"])
            p.wait("gpsimd", "qw0", p.c["qw0"])
            p.wait("gpsimd", "qw1", p.c["qw1"])

        @block.sync
        def _(eng):
            p.begin("sync", eng)
            program(p)

        @block.vector
        def _(eng):
            p.begin("vector", eng)
            program(p)

        @block.scalar
        def _(eng):
            p.begin("scalar", eng)
            program(p)

        @block.tensor
        def _(eng):
            p.begin("tensor", eng)
            program(p)

        @block.gpsimd
        def _(eng):
            p.begin("gpsimd", eng)
            program(p)

    return nc


def prep_weights(Wq, Wk, Wv, Wo, Wg, Wog, Wd, bd, norm_w, gate_W, expert_W, out_W):
    T = lambda a: np.ascontiguousarray(np.asarray(a, np.float32).T)
    return {
        "wq": T(Wq), "wk": T(Wk), "wv": T(Wv), "wg": T(Wg), "wog": T(Wog),
        "wo": T(Wo), "e0": T(expert_W[0]), "e1": T(expert_W[1]), "ow": T(out_W),
        "gw": T(gate_W), "wd": T(Wd),
        "bd": np.ascontiguousarray(np.asarray(bd, np.float32).reshape(-1, 1)),
        "normw_b": np.ascontiguousarray(
            np.broadcast_to(np.asarray(norm_w, np.float32), (128, 1024))),
        "i128": np.eye(128, dtype=np.float32),
        "consts": np.ascontiguousarray(np.broadcast_to(
            np.array([LN_EPS, RMS_EPS, 1.0, 0.0], np.float32), (128, 4))),
        "tri": np.triu(np.ones((128, 128), np.float32)),
        "ones1": np.ones((1, 128), np.float32),
        "ones16": np.ones((16, 128), np.float32),
    }


def make_inputs_per_core(cfg, core, x_full, W):
    NT = cfg.ntok
    d = dict(W)
    d["x"] = np.ascontiguousarray(x_full[core * NT:(core + 1) * NT])
    d["cmask"] = np.full((128, 1), 1.0 if core % 2 == 0 else 0.0, np.float32)
    d["umask"] = np.full((128, 1), 1.0 if core % 2 == 1 else 0.0, np.float32)
    return d


def kernel(L0_attn_x, L0_attn_Wq, L0_attn_Wk, L0_attn_Wv, L0_attn_Wo,
           L0_attn_Wg, L0_attn_Wog, L0_attn_Wd, L0_attn_bd, L0_attn_norm_w,
           L0_ffn_gate_W, L0_ffn_expert_W, L0_ffn_out_W):
    cfg = Cfg(n_cores=8, ntok=4096, G=512)
    x = np.ascontiguousarray(np.asarray(L0_attn_x, np.float32).reshape(-1, 1024))
    W = prep_weights(L0_attn_Wq, L0_attn_Wk, L0_attn_Wv, L0_attn_Wo, L0_attn_Wg,
                     L0_attn_Wog, L0_attn_Wd, L0_attn_bd, L0_attn_norm_w,
                     L0_ffn_gate_W, L0_ffn_expert_W, L0_ffn_out_W)
    nc = build(cfg)
    in_maps = [make_inputs_per_core(cfg, c, x, W) for c in range(cfg.n_cores)]
    res = run_bass_kernel_spmd(nc, in_maps, core_ids=list(range(cfg.n_cores)))
    y = np.concatenate([np.asarray(r["y"]) for r in res.results], axis=0)
    return y.reshape(4, 8192, 1024)
